# version 07
# speedup vs baseline: 1.3706x; 1.0667x over previous
import numpy as np
from contextlib import ExitStack
import concourse.bass as bass
import concourse.mybir as mybir

F32 = mybir.dt.float32
BF16 = mybir.dt.bfloat16
AF = mybir.ActivationFunctionType
ALU = mybir.AluOpType
AX = mybir.AxisListType

ENGS = ("sync", "scalar", "vector", "tensor", "gpsimd")


class XSem:

    def __init__(self, handle, name):
        self.h = handle
        self.name = name
        self.cnt = 0


class Prog:

    def __init__(self, nc, stack):
        self.nc = nc
        self.stack = stack
        self.q = {e: [] for e in ENGS}
        self.sem = {e: stack.enter_context(nc.semaphore(f"c_{e}")) for e in ENGS}
        self.cnt = {e: 0 for e in ENGS}
        self.waited = {e: {} for e in ENGS}

    def xsem(self, name):
        self._xsem_n = getattr(self, "_xsem_n", 0) + 1
        uname = f"{name}_{self._xsem_n}"
        return XSem(self.stack.enter_context(self.nc.semaphore(uname)), uname)

    def _handle(self, key):
        return key.h if isinstance(key, XSem) else self.sem[key]

    def do(self, eng, fn, waits=(), inc=0, sem=None):
        for (k, v) in waits:
            kk = k.name if isinstance(k, XSem) else k
            if v > 0 and self.waited[eng].get(kk, 0) < v:
                self.waited[eng][kk] = v
                self.q[eng].append(("w", self._handle(k), v))
        if inc:
            if sem is not None:
                sem.cnt += inc
                self.q[eng].append(("i", fn, sem.h, inc))
                return sem.cnt
            self.cnt[eng] += inc
            self.q[eng].append(("i", fn, self.sem[eng], inc))
        else:
            assert sem is None
            self.q[eng].append(("o", fn))
        return self.cnt[eng]

    def drain(self, eng, inc=0):
        return self.do(eng, lambda e: e.drain(), inc=inc)

    def tok(self, eng):
        return self.cnt[eng]

    def replay(self, block):
        for ename in ENGS:
            deco = getattr(block, ename)
            q = self.q[ename]

            def body(e, _q=q):
                for item in _q:
                    k = item[0]
                    if k == "w":
                        e.wait_ge(item[1], item[2])
                    elif k == "o":
                        item[1](e)
                    else:
                        item[1](e).then_inc(item[2], item[3])
            deco(body)

    def nops(self):
        return sum(len(q) for q in self.q.values())


class Arena:

    def __init__(self, ap, total, name):
        self.ap = ap
        self.total = total
        self.off = 0
        self.hi = 0
        self.name = name
        self.marks = []

    def take(self, n, tag=""):
        assert self.off + n <= self.total, (
            f"{self.name} overflow: {self.off}+{n} > {self.total} ({tag})")
        r = self.ap[:, self.off:self.off + n]
        self.off += n
        self.hi = max(self.hi, self.off)
        return r

    def push(self):
        self.marks.append(self.off)

    def pop(self):
        self.off = self.marks.pop()


def mm3(P, ps, lh, ll, rh, rl, first, last, waits=(), inc=0):
    P.do("tensor", lambda e, o=ps, a=lh, b=rh, s=first: e.matmul(o, a, b, start=s, stop=False),
         waits=waits)
    P.do("tensor", lambda e, o=ps, a=lh, b=rl: e.matmul(o, a, b, start=False, stop=False))
    return P.do("tensor", lambda e, o=ps, a=ll, b=rh, s=last: e.matmul(o, a, b, start=False, stop=s),
                inc=inc)


class MK:
    def __init__(self, cfg):
        self.cfg = cfg


def build(cfg):
    T = cfg["T"]; D = cfg["D"]
    NCH = D // 128
    assert T % 512 == 0
    nc = bass.Bass("TRN2", target_bir_lowering=False)
    mk = MK(cfg)
    mk.nc = nc
    phases = list(cfg["phases"])

    dt = {}
    def din(name, shape, dtype=F32):
        dt[name] = nc.dram_tensor(name, shape, dtype, kind="ExternalInput").ap()
    def dout(name, shape, dtype=F32):
        dt[name] = nc.dram_tensor(name, shape, dtype, kind="ExternalOutput").ap()
    def dint(name, shape, dtype=F32):
        dt[name] = nc.dram_tensor(name, shape, dtype).ap()

    din("x0", [T, D])
    din("ident", [128, 128])
    din("identb", [128, 128], BF16)
    din("tril", [128, 128])
    dout("y", [T, D])
    if "ffn1" in phases or "ffn2" in phases:
        din("esel", [cfg["NEXP"], cfg["NEXP"], 128])
        din("wr", [D, cfg["NEXP"]])
        din("wsg", [D, cfg["DS"]]); din("wsu", [D, cfg["DS"]]); din("wsd", [cfg["DS"], D])
        din("weg", [cfg["NEXP"], D, cfg["DE"]]); din("weu", [cfg["NEXP"], D, cfg["DE"]])
        din("wed", [cfg["NEXP"], cfg["DE"], D])
    if "attn" in phases:
        S = cfg["S"]; NCORES = cfg["n_cores"]
        din("wq", [2, 128, 128]); din("wdkv", [2, 512, 128])
        din("wuk", [2, 512, 128]); din("wuv", [2, 512, 128])
        din("lngr", [2, 128, 512]); din("lnbr", [2, 128, 512])
        din("woh", [2, 128, 128])
        din("woT", [2, 128, D]); din("bo", [D])
        din("ridx", [1, 1], mybir.dt.int32)
        dint("ag1_in", [2, NCH, 128, T], BF16)
        dint("ag1_out", [NCORES, 2, NCH, 128, T], BF16)
        dint("xh_dram", [2, NCORES, 128, T], BF16)
        dint("o2buf", [2, 2, 128, S], BF16)
        NRSH = 2 if D >= 1024 else 1
        cfg["NRSH"] = NRSH
        dint("rs_in", [NRSH, NCORES * T, D // NRSH])
        dint("rs_out", [NRSH, T, D // NRSH])
        dint("cbuf", [2, 4, 128, S], BF16)
        dint("xspill", [NCH, 128, T])
        if cfg.get("dbg"):
            dt["dbg"] = nc.dram_tensor("dbg", [8, 16, 128, 134], F32, kind="ExternalOutput").ap()
    mk.dt = dt

    with ExitStack() as stack:
        en = stack.enter_context
        AF32_COLS = cfg.get("af32", 23000)
        ABF_COLS = cfg.get("abf", 59000)
        af32 = Arena(en(nc.sbuf_tensor("af32", [128, AF32_COLS], F32)).ap(), AF32_COLS, "af32")
        abf = Arena(en(nc.sbuf_tensor("abf", [128, ABF_COLS], BF16)).ap(), ABF_COLS, "abf")
        mk.af32, mk.abf = af32, abf
        mk.ps = [en(nc.psum_tensor(f"ps{i}", [128, 512], F32)).ap() for i in range(8)]
        P = Prog(nc, stack)
        mk.P = P

        mk.xres = af32.take(NCH * T, "xres")
        mk.ident = af32.take(128, "ident")
        mk.tril = af32.take(128, "tril")
        mk.esel = af32.take(8 * 128, "esel")
        mk.identb = abf.take(128, "identb")
        mk.hh = abf.take(NCH * T, "hh")
        mk.hl = abf.take(NCH * T, "hl")
        mk.frame0_f32 = af32.off
        mk.frame0_bf = abf.off
        mk.wpre = None
        mk.ffn_split_guard = []

        mk.sem_const = P.xsem("ld_const")
        P.do("sync", lambda e: e.dma_start(out=mk.ident, in_=dt["ident"][:, :]), inc=16, sem=mk.sem_const)
        P.do("sync", lambda e: e.dma_start(out=mk.tril, in_=dt["tril"][:, :]), inc=16, sem=mk.sem_const)
        if "esel" in dt:
            P.do("sync", lambda e: e.dma_start(out=mk.esel[0:8, :], in_=dt["esel"][:, :, :]), inc=16, sem=mk.sem_const)
        cdma = P.do("sync", lambda e: e.dma_start(out=mk.identb, in_=dt["identb"][:, :]), inc=16, sem=mk.sem_const)
        if "ridx" in dt:
            mk.ridx_sb = en(nc.sbuf_tensor("ridx_sb", [1, 1], mybir.dt.int32)).ap()
            rdma = P.do("sync", lambda e: e.dma_start(out=mk.ridx_sb[:, :], in_=dt["ridx"][:, :]),
                        inc=16, sem=mk.sem_const)
            def _ldrv(e, a=mk.ridx_sb):
                rv = e.value_load(a[0:1, 0:1])
                mk.rv2 = [e.snap(rv * 2 + 0), e.snap(rv * 2 + 1)]
                return e.nop()
            P.do("gpsimd", _ldrv, waits=[(mk.sem_const, rdma)])
        mk.const_d = (mk.sem_const, mk.sem_const.cnt)

        emit_load_x0(mk)

        for pi, ph in enumerate(phases):
            if ph in ("ffn1", "ffn2"):
                if mk.wpre is None:
                    g = {} if pi == 0 else {
                        "dma": [("vector", mk.x_ready["v"]),
                                ("scalar", mk.x_ready["s"]),
                                ("tensor", mk.x_ready["t"])],
                        "split": [("vector", mk.x_ready["v"]),
                                  ("scalar", mk.x_ready["s"]),
                                  ("tensor", mk.x_ready["t"])]}
                    emit_wpre(mk, guard=g)
                emit_ln(mk)
                emit_ffn(mk, tag=ph)
            elif ph == "attn":
                emit_ln(mk)
                emit_attn(mk, wpre_next=any(
                    p in ("ffn1", "ffn2") for p in phases[pi + 1:]))
            elif ph == "lnonly":
                emit_ln(mk)
                emit_h_to_x(mk)

        emit_store_y(mk)

        block = en(nc.Block())
        P.replay(block)
    mk.sbuf_use = (af32.hi * 4 + abf.hi * 2) / 1024.0
    return nc, mk


def emit_load_x0(mk):
    P, cfg = mk.P, mk.cfg
    T = cfg["T"]; D = cfg["D"]; NCH = D // 128
    nq = T // 128
    af32 = mk.af32
    af32.push()
    tmp = af32.take(2 * D, "x0tm")
    sems = [P.xsem("x0a"), P.xsem("x0b")]
    dma_tok = [0, 0]
    tr_done = {}
    ev_tok = {0: 0, 1: 0}
    for q in range(nq):
        slot = q % 2
        tmv = tmp[:, slot * D:(slot + 1) * D]
        w = [("tensor", tr_done[q - 2])] if q >= 2 else []
        dma_tok[slot] = P.do("sync", lambda e, o=tmv, i=mk.dt["x0"][q * 128:(q + 1) * 128, :]:
                             e.dma_start(out=o, in_=i), waits=w, inc=16, sem=sems[slot])
        for c in range(NCH):
            ps = mk.ps[c % 2]
            tk = P.do("tensor",
                      lambda e, o=ps[:, 0:128], i=tmv[:, c * 128:(c + 1) * 128], idn=mk.ident:
                      e.transpose(o, i, idn),
                      waits=[(sems[slot], dma_tok[slot]), ("vector", ev_tok[c % 2]),
                             (mk.sem_const, mk.const_d[1])], inc=1)
            ev_tok[c % 2] = P.do("vector",
                                 lambda e, o=mk.xres[:, c * T + q * 128: c * T + (q + 1) * 128],
                                 i=ps[:, 0:128]: e.tensor_copy(o, i),
                                 waits=[("tensor", tk)], inc=1)
        tr_done[q] = P.tok("tensor")
    P.drain("vector", inc=1)
    mk.x_ready = {"v": P.tok("vector"), "t": P.tok("tensor"), "s": P.tok("scalar")}
    af32.pop()


def emit_ln(mk):
    P, cfg = mk.P, mk.cfg
    T = cfg["T"]; D = cfg["D"]; NCH = D // 128
    nq = T // 128
    nbn = (D + 511) // 512
    af32 = mk.af32
    af32.push()
    xtm = af32.take(2 * D, "ln_xtm")
    sb = af32.take(32, "ln_sb")
    xr = mk.x_ready
    V = "vector"
    evA = {0: ("vector", 0), 1: ("vector", 0)}
    evB = {0: [("vector", 0)], 1: [("vector", 0)]}
    ap_guard = {0: 0, 1: 0}
    for q in range(nq):
        slot = q % 2
        xv = xtm[:, slot * D:(slot + 1) * D]
        hv = xv
        for c in range(NCH):
            ps = mk.ps[c % 2]
            tk = P.do("tensor", lambda e, o=ps[:, 0:128],
                      i=mk.xres[:, c * T + q * 128: c * T + (q + 1) * 128], idn=mk.ident:
                      e.transpose(o, i, idn),
                      waits=[("vector", xr["v"]), ("tensor", xr["t"]), evA[c % 2]], inc=1)
            ev = P.do(V, lambda e, o=xv[:, c * 128:(c + 1) * 128], i=ps[:, 0:128]:
                      e.tensor_copy(o, i), waits=[("tensor", tk)], inc=1)
            evA[c % 2] = ("vector", ev)
        P.drain(V, inc=1)
        for j in range(nbn):
            w = min(512, D - j * 512)
            P.do(V, lambda e, o=sb[:, j * 6:(j + 1) * 6], i=xv[:, j * 512:j * 512 + w]:
                 e.bn_stats(o, i))
        P.drain(V)
        P.do(V, lambda e, o=sb[:, 24:26], i=sb[:, 0:6 * nbn]: e.bn_aggr(o, i))
        P.drain(V)
        P.do(V, lambda e, o=sb[:, 26:27], i=sb[:, 25:26]:
             e.tensor_scalar(o, i, 1e-5, None, ALU.add))
        P.drain(V, inc=1)
        vpe_tok = P.tok(V)
        sq_tok = P.do("scalar", lambda e, o=sb[:, 27:28], i=sb[:, 26:27]:
                      e.activation(o, i, AF.Sqrt), waits=[("vector", vpe_tok)], inc=1)
        P.do(V, lambda e, o=sb[:, 28:29], i=sb[:, 27:28]: e.reciprocal(o, i),
             waits=[("scalar", sq_tok)])
        P.drain(V)
        P.do(V, lambda e, o=hv, i=xv, m=sb[:, 24:25], s=sb[:, 28:29]:
             e.tensor_scalar(o, i, m, s, ALU.subtract, ALU.mult),
             waits=[("scalar", ap_guard[slot])])
        P.drain(V, inc=1)
        ap_tok = P.tok(V)
        hi_tok = 0
        for c in range(NCH):
            ps = mk.ps[2 + c % 2]
            tk = P.do("tensor", lambda e, o=ps[:, 0:128], i=hv[:, c * 128:(c + 1) * 128],
                      idn=mk.ident: e.transpose(o, i, idn),
                      waits=[("vector", ap_tok)] + evB[c % 2], inc=1)
            hi_tok = P.do("scalar", lambda e, o=mk.hh[:, c * T + q * 128: c * T + (q + 1) * 128],
                          i=ps[:, 0:128]: e.activation(o, i, AF.Copy),
                          waits=[("tensor", tk)], inc=1)
            lo_tok = P.do(V, lambda e, o=mk.hl[:, c * T + q * 128: c * T + (q + 1) * 128],
                          a=ps[:, 0:128], b=mk.hh[:, c * T + q * 128: c * T + (q + 1) * 128]:
                          e.tensor_tensor(o, a, b, ALU.subtract),
                          waits=[("scalar", hi_tok)], inc=1)
            evB[c % 2] = [("scalar", hi_tok), ("vector", lo_tok)]
        ap_guard[slot] = hi_tok
    P.drain(V, inc=1)
    mk.h_ready = {"v": P.tok("vector"), "s": P.tok("scalar")}
    af32.pop()


def emit_h_to_x(mk):
    P, cfg = mk.P, mk.cfg
    T = cfg["T"]; NCH = cfg["D"] // 128
    for c in range(NCH):
        P.do("vector", lambda e, o=mk.xres[:, c * T:(c + 1) * T],
             a=mk.hh[:, c * T:(c + 1) * T], b=mk.hl[:, c * T:(c + 1) * T]:
             e.tensor_tensor(o, a, b, ALU.add),
             waits=[("vector", mk.h_ready["v"]), ("scalar", mk.h_ready["s"])], inc=1)
    P.drain("vector", inc=1)
    mk.x_ready = {"v": P.tok("vector"), "t": P.tok("tensor"), "s": P.tok("scalar")}


def _ffn_wraw_pad(cfg):
    T = cfg["T"]; TT_ = T // 512; NEXP = cfg["NEXP"]
    return 2 * TT_ * 512 + (T // 128) * NEXP + T + T + 16 + 8


def emit_wpre(mk, guard=None):
    P, cfg = mk.P, mk.cfg
    guard = guard or {}
    assert cfg["DS"] >= 256
    NCH = cfg["D"] // 128
    n = min(4, NCH)
    pad = _ffn_wraw_pad(cfg)
    wraw = mk.af32.ap[:, mk.frame0_f32 + pad: mk.frame0_f32 + pad + 1024]
    whil = mk.abf.ap[:, mk.frame0_bf: mk.frame0_bf + 2048]
    pre = dict(n=n, wsem=[P.xsem(f"w{i}") for i in range(8)], hi=[], lo=[])
    sg = list(guard.get("split", []))
    mk.ffn_split_guard = sg
    for k in range(n):
        rawv = wraw[:, k * 256: k * 256 + 256]
        dma_tok = P.do("sync", lambda e, o=rawv, i=mk.dt["wsg"][k * 128:(k + 1) * 128, 0:256]:
                       e.dma_start(out=o, in_=i), waits=list(guard.get("dma", [])),
                       inc=16, sem=pre["wsem"][k])
        hiv = whil[:, k * 512: k * 512 + 256]
        lov = whil[:, k * 512 + 256: k * 512 + 512]
        hi_tok = P.do("scalar", lambda e, o=hiv, i=rawv: e.activation(o, i, AF.Copy),
                      waits=[(pre["wsem"][k], dma_tok)] + sg, inc=1)
        lo_tok = P.do("vector", lambda e, o=lov, a=rawv, b=hiv:
                      e.tensor_tensor(o, a, b, ALU.subtract),
                      waits=[(pre["wsem"][k], dma_tok), ("scalar", hi_tok)] + sg, inc=1)
        pre["hi"].append(hi_tok)
        pre["lo"].append(lo_tok)
    mk.wpre = pre


class FFNState:
    def __init__(self, mk):
        self.mk = mk
        self.bank_free = [("vector", 0)] * 8
        self.wbuf_free = [0] * 8
        self.rawbuf_free_s = [0] * 8
        self.rawbuf_free_v = [0] * 8
        self.setsel = 0
        self.combrep_tok = 0
        self.pending = None
        self.sge_free = ("vector", 0)
        self.he_done = []
        self.pre = None


def _stream_mm(mk, st, w_dram_ap, kchunks, cols_rng, rhs_hi, rhs_lo, banks,
               n_m, t_tiles, extra_mm_waits=(), dma_guard=(), deferred=None,
               pre=None):
    P = mk.P
    c0, c1 = cols_rng
    gcols = c1 - c0
    wraw, whil = st.wraw, st.whil
    assert gcols <= 256
    nslots = 4
    if pre is not None:
        assert c0 == 0 and gcols == 256 and kchunks >= pre["n"]
    for k in range(kchunks):
        slot = k % nslots
        rawv = wraw[:, slot * 256: slot * 256 + gcols]
        hiv = whil[:, slot * 512: slot * 512 + gcols]
        lov = whil[:, slot * 512 + 256: slot * 512 + 256 + gcols]
        if pre is not None and k < pre["n"]:
            hi_tok, lo_tok = pre["hi"][k], pre["lo"][k]
            st.rawbuf_free_s[slot] = max(st.rawbuf_free_s[slot], hi_tok)
            st.rawbuf_free_v[slot] = max(st.rawbuf_free_v[slot], lo_tok)
        else:
            dma_tok = P.do("sync", lambda e, o=rawv, i=w_dram_ap[k * 128:(k + 1) * 128, c0:c1]:
                           e.dma_start(out=o, in_=i),
                           waits=[("scalar", st.rawbuf_free_s[slot]),
                                  ("vector", st.rawbuf_free_v[slot])] + list(dma_guard if k < 2 else ()),
                           inc=16, sem=st.wsem[slot])
            hi_tok = P.do("scalar", lambda e, o=hiv, i=rawv: e.activation(o, i, AF.Copy),
                          waits=[(st.wsem[slot], dma_tok), ("tensor", st.wbuf_free[slot])], inc=1)
            st.rawbuf_free_s[slot] = hi_tok
            lo_tok = P.do("vector", lambda e, o=lov, a=rawv, b=hiv: e.tensor_tensor(o, a, b, ALU.subtract),
                          waits=[(st.wsem[slot], dma_tok), ("scalar", hi_tok), ("tensor", st.wbuf_free[slot])],
                          inc=1)
            st.rawbuf_free_v[slot] = lo_tok
        if deferred is not None and k == min(1, kchunks - 1):
            deferred()
            deferred = None
        mmw = [("scalar", hi_tok), ("vector", lo_tok)] + (list(extra_mm_waits) if k == 0 else [])
        first = (k == 0)
        last = (k == kchunks - 1)
        for mi in range(n_m):
            lh = hiv[:, mi * 128:(mi + 1) * 128]
            ll = lov[:, mi * 128:(mi + 1) * 128]
            for ti in range(t_tiles):
                P.do("tensor", lambda e, o=banks[(mi, ti)], a=lh, b=rhs_hi(k, ti), s=first:
                     e.matmul(o, a, b, start=s, stop=False), waits=mmw)
                mmw = ()
                P.do("tensor", lambda e, o=banks[(mi, ti)], a=lh, b=rhs_lo(k, ti):
                     e.matmul(o, a, b, start=False, stop=False))
            for ti in range(t_tiles):
                is_last_of_chunk = (mi == n_m - 1 and ti == t_tiles - 1)
                P.do("tensor", lambda e, o=banks[(mi, ti)], a=ll, b=rhs_hi(k, ti), q=last:
                     e.matmul(o, a, b, start=False, stop=q),
                     inc=(1 if is_last_of_chunk else 0))
        st.wbuf_free[slot] = P.tok("tensor")
    return P.tok("tensor")


def _bank_waits(st, idxs):
    w = []
    for i in idxs:
        eng, tok = st.bank_free[i]
        if tok:
            w.append((eng, tok))
    return w


def emit_ffn(mk, tag):
    P, cfg = mk.P, mk.cfg
    T = cfg["T"]; D = cfg["D"]; DE = cfg["DE"]; DS = cfg["DS"]; NEXP = cfg["NEXP"]
    NCH = D // 128
    TT_ = T // 512
    af32, abf = mk.af32, mk.abf
    af32.push(); abf.push()

    assert abf.off == mk.frame0_bf
    whil = abf.take(2 * 1024, f"whilo")
    sge = af32.take(2 * TT_ * 512, f"sge")
    mx_ech = min(768, max(DE, DS)) // 128
    hehi = abf.take(mx_ech * T, f"hehi")
    helo = abf.take(mx_ech * T, f"helo")
    comb = af32.take((T // 128) * NEXP, f"comb")
    combT = af32.take(T, f"combT")
    combrep = af32.take(T, f"combrep")
    rtmp = af32.take(16, f"rtmp")
    rstat = af32.take(8, f"rstat")
    assert af32.off == mk.frame0_f32 + _ffn_wraw_pad(cfg)
    wraw = af32.take(2 * 512, f"wraw")

    pre = mk.wpre
    mk.wpre = None
    st = FFNState(mk)
    st.wraw, st.whil = wraw, whil
    st.wsem = pre["wsem"] if pre else [P.xsem(f"w{i}") for i in range(8)]
    st.pre = pre
    hrv, hrs = mk.h_ready["v"], mk.h_ready["s"]
    guard = [("vector", hrv), ("scalar", hrs)]
    rdguard = ([("scalar", pre["hi"][0]), ("vector", pre["lo"][0])]
               if pre else guard)

    emit_router(mk, st, comb, combT, rtmp, rstat, guard, rdguard)

    def cuts(total, maxc=768):
        cs = list(range(0, total, maxc)) + [total]
        return [(cs[i], cs[i + 1]) for i in range(len(cs) - 1) if cs[i] < cs[i + 1]]

    units = []
    for (a, b) in cuts(DS):
        units.append(dict(gw="wsg", uw="wsu", dw="wsd", ecols=(a, b), eidx=None))
    for eix in range(NEXP):
        for (a, b) in cuts(DE):
            units.append(dict(gw="weg", uw="weu", dw="wed", ecols=(a, b), eidx=eix))

    for u in units:
        emit_ffn_unit(mk, u, st, sge, combT, combrep, hehi, helo, guard)
    if st.pending is not None:
        cb = st.pending
        st.pending = None
        cb()

    P.drain("vector", inc=1)
    mk.x_ready = {"v": P.tok("vector"), "t": P.tok("tensor"), "s": P.tok("scalar")}
    af32.pop(); abf.pop()


def emit_ffn_unit(mk, u, st, sge, combT, combrep, hehi, helo, guard):
    P, cfg = mk.P, mk.cfg
    T = cfg["T"]; D = cfg["D"]
    NCH = D // 128
    TT_ = T // 512
    e0, e1 = u["ecols"]
    nech = (e1 - e0) // 128
    eix = u["eidx"]

    def wap(name):
        a = mk.dt[name]
        return a[eix] if eix is not None else a

    def flush():
        if st.pending is not None:
            cb = st.pending
            st.pending = None
            cb()

    if eix is not None and e0 == 0:
        flush()
        bs = st.setsel * 4
        for h in range(TT_):
            ps = mk.ps[bs + h]
            tk = P.do("tensor", lambda e, o=ps, a=mk.esel[0:8, eix * 128:(eix + 1) * 128],
                      b=combT[0:8, h * 512:(h + 1) * 512]:
                      e.matmul(o, a, b, start=True, stop=True),
                      waits=_bank_waits(st, [bs + h]) + [("vector", mk.combT_tok)], inc=1)
            ctok = P.do("vector", lambda e, o=combrep[:, h * 512:(h + 1) * 512], i=ps:
                        e.tensor_copy(o, i), waits=[("tensor", tk)], inc=1)
            st.bank_free[bs + h] = ("vector", ctok)
        P.drain("vector", inc=1)
        st.combrep_tok = P.tok("vector")
        st.setsel ^= 1
    if eix is None:
        st.combrep_tok = 0

    mgroups = [(g0, min(2, nech - g0)) for g0 in range(0, nech, 2)]
    for gi, (g0, nm) in enumerate(mgroups):
        bsA = st.setsel * 4
        banksA = {(mi, ti): mk.ps[bsA + mi * TT_ + ti] for mi in range(nm) for ti in range(TT_)}
        gw = _bank_waits(st, [bsA + i for i in range(nm * TT_)]) + guard
        pre_arg = st.pre
        st.pre = None
        t_gate = _stream_mm(mk, st, wap(u["gw"]), NCH, (e0 + g0 * 128, e0 + (g0 + nm) * 128),
                            lambda k, ti: mk.hh[:, k * T + ti * 512: k * T + ti * 512 + 512],
                            lambda k, ti: mk.hl[:, k * T + ti * 512: k * T + ti * 512 + 512],
                            banksA, nm, TT_, extra_mm_waits=gw, dma_guard=(),
                            deferred=st.pending, pre=pre_arg)
        st.pending = None
        holder = {}

        def silu_cb(banksA=banksA, t_gate=t_gate, bsA=bsA, nm=nm, holder=holder):
            tok = 0
            for mi in range(nm):
                for ti in range(TT_):
                    sl = mi * TT_ + ti
                    tok = P.do("scalar", lambda e, o=sge[:, sl * 512:(sl + 1) * 512],
                               i=banksA[(mi, ti)]: e.activation(o, i, AF.Silu),
                               waits=[("tensor", t_gate), st.sge_free], inc=1)
            for i in range(nm * TT_):
                st.bank_free[bsA + i] = ("scalar", tok)
            holder["silu"] = tok

        st.setsel ^= 1
        bsB = st.setsel * 4
        banksB = {(mi, ti): mk.ps[bsB + mi * TT_ + ti] for mi in range(nm) for ti in range(TT_)}
        uw = _bank_waits(st, [bsB + i for i in range(nm * TT_)])
        t_up = _stream_mm(mk, st, wap(u["uw"]), NCH, (e0 + g0 * 128, e0 + (g0 + nm) * 128),
                          lambda k, ti: mk.hh[:, k * T + ti * 512: k * T + ti * 512 + 512],
                          lambda k, ti: mk.hl[:, k * T + ti * 512: k * T + ti * 512 + 512],
                          banksB, nm, TT_, extra_mm_waits=uw, dma_guard=(),
                          deferred=silu_cb)

        def he_cb(banksB=banksB, t_up=t_up, bsB=bsB, nm=nm, g0=g0, holder=holder):
            V = "vector"
            for mi in range(nm):
                for ti in range(TT_):
                    sl = mi * TT_ + ti
                    P.do(V, lambda e, o=sge[:, sl * 512:(sl + 1) * 512],
                         a=sge[:, sl * 512:(sl + 1) * 512], b=banksB[(mi, ti)]:
                         e.tensor_tensor(o, a, b, ALU.mult),
                         waits=[("tensor", t_up), ("scalar", holder["silu"]),
                                ("vector", st.combrep_tok)], inc=1)
            P.drain(V, inc=1)
            het_tok = P.tok(V)
            for i in range(nm * TT_):
                st.bank_free[bsB + i] = ("vector", het_tok)
            if eix is not None:
                for mi in range(nm):
                    for ti in range(TT_):
                        sl = mi * TT_ + ti
                        P.do(V, lambda e, o=sge[:, sl * 512:(sl + 1) * 512],
                             a=sge[:, sl * 512:(sl + 1) * 512],
                             b=combrep[:, ti * 512:(ti + 1) * 512]:
                             e.tensor_tensor(o, a, b, ALU.mult))
                P.drain(V, inc=1)
                het_tok = P.tok(V)
            hih = 0
            for mi in range(nm):
                for ti in range(TT_):
                    sl = mi * TT_ + ti
                    hih = P.do("scalar", lambda e,
                               o=hehi[:, (g0 + mi) * T + ti * 512:(g0 + mi) * T + ti * 512 + 512],
                               i=sge[:, sl * 512:(sl + 1) * 512]: e.activation(o, i, AF.Copy),
                               waits=[("vector", het_tok)], inc=1)
            lot = 0
            for mi in range(nm):
                for ti in range(TT_):
                    sl = mi * TT_ + ti
                    lot = P.do(V, lambda e,
                               o=helo[:, (g0 + mi) * T + ti * 512:(g0 + mi) * T + ti * 512 + 512],
                               a=sge[:, sl * 512:(sl + 1) * 512],
                               b=hehi[:, (g0 + mi) * T + ti * 512:(g0 + mi) * T + ti * 512 + 512]:
                               e.tensor_tensor(o, a, b, ALU.subtract),
                               waits=[("scalar", hih)], inc=1)
            P.drain(V, inc=1)
            st.sge_free = ("vector", P.tok(V))
            st.he_done = [("vector", P.tok(V)), ("scalar", hih)]

        if gi == len(mgroups) - 1:
            he_cb()
        else:
            st.pending = he_cb
        st.setsel ^= 1

    he_waits = st.he_done

    for g0 in range(0, NCH, 2):
        bs = st.setsel * 4
        banks = {(mi, ti): mk.ps[bs + mi * TT_ + ti] for mi in range(2) for ti in range(TT_)}
        dwt = _bank_waits(st, [bs + i for i in range(2 * TT_)]) + he_waits
        t_down = _stream_mm(mk, st, wap(u["dw"])[e0:e1, :], nech, (g0 * 128, (g0 + 2) * 128),
                            lambda k, ti: hehi[:, k * T + ti * 512: k * T + ti * 512 + 512],
                            lambda k, ti: helo[:, k * T + ti * 512: k * T + ti * 512 + 512],
                            banks, 2, TT_, extra_mm_waits=dwt, dma_guard=(),
                            deferred=st.pending)
        st.pending = None

        def down_cb(banks=banks, t_down=t_down, bs=bs, g0=g0):
            for mi in range(2):
                for ti in range(TT_):
                    c = g0 + mi
                    acc = mk.xres[:, c * T + ti * 512: c * T + ti * 512 + 512]
                    ev = P.do("vector", lambda e, o=acc, i=banks[(mi, ti)], a=acc:
                              e.scalar_tensor_tensor(o, i, 0.5, a, ALU.mult, ALU.add),
                              waits=[("tensor", t_down)], inc=1)
                    st.bank_free[bs + mi * TT_ + ti] = ("vector", ev)
            P.drain("vector", inc=1)

        st.pending = down_cb
        st.setsel ^= 1


def emit_router(mk, st, comb, combT, rtmp, rstat, guard, rdguard):
    P, cfg = mk.P, mk.cfg
    T = cfg["T"]; D = cfg["D"]; NEXP = cfg["NEXP"]
    NCH = D // 128
    nq = T // 128
    nhalf = T // 512
    V = "vector"
    abf = mk.abf
    abf.push()
    wr_hi = abf.take(NCH * NEXP, "wrhi")
    wr_lo = abf.take(NCH * NEXP, "wrlo")
    rraw = st.wraw[:, 0:NCH * NEXP]
    d_tok = 0
    for c in range(NCH):
        d_tok = P.do("sync", lambda e, o=rraw[:, c * NEXP:(c + 1) * NEXP],
                     i=mk.dt["wr"][c * 128:(c + 1) * 128, :]:
                     e.dma_start(out=o, in_=i), waits=rdguard, inc=16, sem=st.wsem[0])
    h_tok = P.do("scalar", lambda e, o=wr_hi, i=rraw: e.activation(o, i, AF.Copy),
                 waits=[(st.wsem[0], d_tok)] + mk.ffn_split_guard, inc=1)
    l_tok = P.do("vector", lambda e, o=wr_lo, a=rraw, b=wr_hi: e.tensor_tensor(o, a, b, ALU.subtract),
                 waits=[("scalar", h_tok)] + mk.ffn_split_guard, inc=1)
    st.rawbuf_free_s[0] = h_tok
    st.rawbuf_free_v[0] = l_tok
    for h in range(nhalf):
        for k in range(NCH):
            mm3(P, mk.ps[h][0:NEXP, :], wr_hi[:, k * NEXP:(k + 1) * NEXP],
                wr_lo[:, k * NEXP:(k + 1) * NEXP],
                mk.hh[:, k * T + h * 512: k * T + h * 512 + 512],
                mk.hl[:, k * T + h * 512: k * T + h * 512 + 512],
                first=(k == 0), last=(k == NCH - 1),
                waits=([("vector", l_tok), ("scalar", h_tok)] + guard) if k == 0 else ())
    r_tok = P.drain("tensor", inc=1)
    ev = 0
    for h in range(nhalf):
        ev = P.do(V, lambda e, o=combT[0:NEXP, h * 512:(h + 1) * 512], i=mk.ps[h][0:NEXP, :]:
                  e.tensor_copy(o, i), waits=[("tensor", r_tok)], inc=1)
    P.drain(V, inc=1)
    rT_tok = P.tok(V)
    tr_ev = {0: 0, 1: 0}
    for q in range(nq):
        ps = mk.ps[2 + (q % 2)]
        tk = P.do("tensor", lambda e, o=ps[:, 0:NEXP], i=combT[0:NEXP, q * 128:(q + 1) * 128],
                  idn=mk.ident[0:NEXP, 0:NEXP]: e.transpose(o, i, idn),
                  waits=[("vector", rT_tok), ("vector", tr_ev[q % 2])], inc=1)
        tr_ev[q % 2] = P.do(V, lambda e, o=comb[:, q * NEXP:(q + 1) * NEXP], i=ps[:, 0:NEXP]:
                            e.tensor_copy(o, i), waits=[("tensor", tk)], inc=1)
    P.drain(V, inc=1)
    M1, M2, RC = 0, 1, 2
    for q in range(nq):
        cm = comb[:, q * NEXP:(q + 1) * NEXP]
        tm = rtmp[:, 0:NEXP]
        P.do(V, lambda e, o=rstat[:, M1:M1 + 1], i=cm: e.reduce_max(o, i, axis=AX.X, negate=True))
        P.drain(V, inc=1)
        nmax_tok = P.tok(V)
        ex_tok = P.do("scalar", lambda e, o=cm, i=cm, b=rstat[:, M1:M1 + 1], s=rstat[:, RC:RC + 1]:
                      e.activation(o, i, AF.Exp, bias=b, accum_out=s),
                      waits=[("vector", nmax_tok)], inc=1)
        P.do(V, lambda e, o=rstat[:, RC:RC + 1], i=rstat[:, RC:RC + 1]: e.reciprocal(o, i),
             waits=[("scalar", ex_tok)])
        P.drain(V)
        P.do(V, lambda e, o=cm, i=cm, s=rstat[:, RC:RC + 1]: e.tensor_scalar(o, i, s, None, ALU.mult))
        P.drain(V)
        P.do(V, lambda e, o=rstat[:, M1:M1 + 1], i=cm: e.reduce_max(o, i, axis=AX.X))
        P.drain(V)
        P.do(V, lambda e, o=tm, i=cm, s=rstat[:, M1:M1 + 1]: e.tensor_scalar(o, i, s, None, ALU.is_ge))
        P.drain(V)
        P.do(V, lambda e, o=tm, i=tm, b=cm: e.scalar_tensor_tensor(o, i, -1e30, b, ALU.mult, ALU.add))
        P.drain(V)
        P.do(V, lambda e, o=rstat[:, M2:M2 + 1], i=tm: e.reduce_max(o, i, axis=AX.X))
        P.drain(V)
        P.do(V, lambda e, o=tm, i=cm, s=rstat[:, M2:M2 + 1]: e.tensor_scalar(o, i, s, None, ALU.is_ge))
        P.drain(V)
        P.do(V, lambda e, o=cm, a=cm, b=tm: e.tensor_tensor(o, a, b, ALU.mult))
        P.drain(V, inc=1)
    comb_tok = P.tok(V)
    tr_ev = {0: 0, 1: 0}
    for q in range(nq):
        ps = mk.ps[2 + (q % 2)]
        tk = P.do("tensor", lambda e, o=ps[0:NEXP, 0:128], i=comb[:, q * NEXP:(q + 1) * NEXP],
                  idn=mk.ident: e.transpose(o, i, idn),
                  waits=[("vector", comb_tok), ("vector", tr_ev[q % 2])], inc=1)
        tr_ev[q % 2] = P.do(V, lambda e, o=combT[0:NEXP, q * 128:(q + 1) * 128],
                            i=ps[0:NEXP, 0:128]: e.tensor_copy(o, i),
                            waits=[("tensor", tk)], inc=1)
    P.drain(V, inc=1)
    mk.combT_tok = P.tok(V)
    for b in set(list(range(nhalf)) + [2, 3]):
        st.bank_free[b] = ("vector", mk.combT_tok)
    abf.pop()


def emit_attn(mk, wpre_next=False):
    P, cfg = mk.P, mk.cfg
    T = cfg["T"]; D = cfg["D"]; S = cfg["S"]; NCO = cfg["n_cores"]
    NCH = D // 128
    LCH = 4
    GQT = min(16, S // 128)
    V = "vector"
    hrv, hrs = mk.h_ready["v"], mk.h_ready["s"]
    af32, abf = mk.af32, mk.abf
    af32.push(); abf.push()

    QTW = LCH * GQT * 128
    if NCH * T >= 2 * S:
        xh_hi, xh_lo = mk.hh[:, 0:S], mk.hh[:, S:2 * S]
    else:
        xh_hi, xh_lo = abf.take(S, "xh_hi"), abf.take(S, "xh_lo")
    if NCH * T >= 2 * QTW:
        qt_hi, qt_lo = mk.hl[:, 0:QTW], mk.hl[:, QTW:2 * QTW]
    else:
        qt_hi, qt_lo = abf.take(QTW, "qt_hi"), abf.take(QTW, "qt_lo")
    FW = S + 4 * 1024 + GQT * 128
    if NCH * T >= FW:
        fs = mk.xres
    else:
        fs = af32.take(FW, "attn_f32")
    vbuf = fs[:, 0:S]
    span = fs[:, S:S + 1024]
    epan = fs[:, S + 1024:S + 2048]
    ohacc = fs[:, S + 2048:S + 2048 + GQT * 128]
    sp2o = S + 2048 + GQT * 128
    span2 = fs[:, sp2o:sp2o + 1024]
    epan2 = fs[:, sp2o + 1024:sp2o + 2048]
    R = dict(
        xh_hi=xh_hi, xh_lo=xh_lo, qt_hi=qt_hi, qt_lo=qt_lo,
        vbuf=vbuf, span=span, epan=epan, span2=span2, epan2=epan2, ohacc=ohacc,
        fsS=fs[:, S:S + 2048],
        cwork=af32.take(4 * 512, "cwork"),
        wwork=af32.take(3 * 512, "wwork"),
        astat=af32.take(64, "astat"),
        cstage=abf.take(2 * LCH * 512, "cstage"),
        cpan=[dict(hi=abf.take(LCH * 1024, f"cp{s}h"), lo=abf.take(LCH * 1024, f"cp{s}l"))
              for s in range(2)],
        whi=abf.take(1664, "awhi"),
        wlo=abf.take(1664, "awlo"),
        o2st=abf.take(2 * min(8, GQT) * 128, "o2st"),
    )
    R["cstage_hi"] = R["cstage"][:, 0:LCH * 512]
    R["cstage_lo"] = R["cstage"][:, LCH * 512:2 * LCH * 512]
    rg = [list(range(NCO))]
    dtb = mk.dt

    mk.attn_ps_guard = {}
    mk.attn_xh_guard = []
    mk.attn_qt_guard = []
    mk.attn_head_guard = {0: [("vector", mk.x_ready["v"]), ("scalar", mk.x_ready["s"])], 1: []}

    ag1w = P.xsem("ag1w")
    for hl, src in ((0, mk.hh), (1, mk.hl)):
        for c in range(NCH):
            P.do("gpsimd", lambda e, o=dtb["ag1_in"][hl, c],
                 i=src[:, c * T:(c + 1) * T]: e.dma_start(out=o, in_=i),
                 waits=[("vector", hrv), ("scalar", hrs)], inc=16, sem=ag1w)
    spx = P.xsem("spx")
    for c in range(NCH):
        P.do("sync", lambda e, o=dtb["xspill"][c], i=mk.xres[:, c * T:(c + 1) * T]:
             e.dma_start(out=o, in_=i), waits=[("vector", mk.x_ready["v"])], inc=16, sem=spx)
    mk.spx = spx
    mk.attn_xh_guard = [(ag1w, ag1w.cnt)]
    mk.attn_head_guard[0] += [(spx, spx.cnt), (ag1w, ag1w.cnt)]

    cc = P.xsem("cc1")
    P.do("gpsimd", lambda e: e.collective_compute(
        "AllGather", ALU.bypass, replica_groups=rg,
        ins=[dtb["ag1_in"]], outs=[dtb["ag1_out"]]),
        waits=[(ag1w, ag1w.cnt)], inc=1, sem=cc)

    mk.dbg_sem = P.xsem("dbgd") if cfg.get("dbg") else None
    mk.attn_sb_dma_reads = []
    o2w = P.xsem("o2w")
    for hp in range(2):
        _attn_head(mk, hp, cc, o2w, R)

    _emit_wo_partial(mk, R, o2w)
    cc2 = P.xsem("cc2")
    for hf in range(cfg["NRSH"]):
        P.do("gpsimd", lambda e, h=hf: e.collective_compute(
            "ReduceScatter", ALU.add, replica_groups=rg,
            ins=[dtb["rs_in"][h]], outs=[dtb["rs_out"][h]]),
            waits=mk.rs_done, inc=1, sem=cc2)
    mk.cc2 = cc2
    mk.attn_done = {"v": P.tok("vector"), "s": P.tok("scalar"), "t": P.tok("tensor")}
    af32.pop(); abf.pop()
    if wpre_next:
        ag = ([("tensor", mk.attn_done["t"]), ("vector", mk.attn_done["v"]),
               ("scalar", mk.attn_done["s"])] + mk.rs_done
              + mk.attn_sb_dma_reads)
        emit_wpre(mk, guard={"dma": ag, "split": ag})
    emit_wo_final(mk)


def _attn_head(mk, hp, cc, ag2w, R):
    P, cfg = mk.P, mk.cfg
    T = cfg["T"]; S = cfg["S"]; NCO = cfg["n_cores"]
    LCH = 4
    GQT = min(16, S // 128)
    NGRP = S // (GQT * 128)
    NKT = S // 128
    V = "vector"
    dtb = mk.dt
    SC = float(1.0 / np.sqrt(512.0))
    whi, wlo, wwork, cwork, astat = R["whi"], R["wlo"], R["wwork"], R["cwork"], R["astat"]
    csh, csl = R["cstage_hi"], R["cstage_lo"]

    wld = P.xsem(f"wld{hp}")
    guard0 = mk.attn_head_guard.get(hp, [])

    def split_from_psum(ps, dst_hi, dst_lo, tk, w=128):
        hi = P.do("scalar", lambda e, o=dst_hi, i=ps[:, 0:w]: e.activation(o, i, AF.Copy),
                  waits=[("tensor", tk)], inc=1)
        lo = P.do(V, lambda e, o=dst_lo, a=ps[:, 0:w], b=dst_hi:
                  e.tensor_tensor(o, a, b, ALU.subtract), waits=[("scalar", hi)], inc=1)
        return hi, lo

    tok = 0
    for lc in range(4):
        tok = P.do("sync", lambda e, o=wwork[:, lc * 128:(lc + 1) * 128],
                   i=dtb["wdkv"][hp][lc * 128:(lc + 1) * 128, :]: e.dma_start(out=o, in_=i),
                   waits=[(wld, wld.cnt)] + guard0, inc=16, sem=wld)
    hi_t = lo_t = 0
    for lc in range(4):
        ps = mk.ps[4 + lc % 2]
        tk = P.do("tensor", lambda e, o=ps[:, 0:128], i=wwork[:, lc * 128:(lc + 1) * 128],
                  idn=mk.ident: e.transpose(o, i, idn),
                  waits=[(wld, tok), (mk.sem_const, mk.const_d[1])] + mk.attn_ps_guard.get(4 + lc % 2, []),
                  inc=1)
        hi_t, lo_t = split_from_psum(ps, whi[:, lc * 128:(lc + 1) * 128],
                                     wlo[:, lc * 128:(lc + 1) * 128], tk)
        mk.attn_ps_guard[4 + lc % 2] = [("scalar", hi_t), ("vector", lo_t)]
    tok2 = P.do("sync", lambda e, o=wwork[:, 512:640], i=dtb["wq"][hp]:
                e.dma_start(out=o, in_=i),
                waits=[(wld, wld.cnt), ("scalar", hi_t), ("vector", lo_t)],
                inc=16, sem=wld)
    for lc in range(4):
        tok2 = P.do("sync", lambda e, o=wwork[:, 640 + lc * 128: 768 + lc * 128],
                    i=dtb["wuk"][hp][lc * 128:(lc + 1) * 128, :]: e.dma_start(out=o, in_=i),
                    waits=[(wld, wld.cnt)], inc=16, sem=wld)
    ps = mk.ps[6]
    tk = P.do("tensor", lambda e, o=ps[:, 0:128], i=wwork[:, 512:640], idn=mk.ident:
              e.transpose(o, i, idn), waits=[(wld, tok2)] + mk.attn_ps_guard.get(6, []), inc=1)
    qh, ql = split_from_psum(ps, csh[:, 0:128], csl[:, 0:128], tk)
    qh2, ql2 = qh, ql
    for lc in range(4):
        ps = mk.ps[7] if lc % 2 else mk.ps[6]
        tk = P.do("tensor", lambda e, o=ps[:, 0:128], i=wwork[:, 640 + lc * 128:768 + lc * 128],
                  idn=mk.ident: e.transpose(o, i, idn),
                  waits=[(wld, tok2), ("scalar", qh), ("vector", ql)]
                  + mk.attn_ps_guard.get(7 if lc % 2 else 6, []), inc=1)
        qh2, ql2 = split_from_psum(ps, csh[:, 128 + lc * 128:256 + lc * 128],
                                   csl[:, 128 + lc * 128:256 + lc * 128], tk)
        mk.attn_ps_guard[7 if lc % 2 else 6] = [("scalar", qh2), ("vector", ql2)]
    ps_a = mk.ps[5]
    mm3(P, ps_a, csh[:, 0:128], csl[:, 0:128], csh[:, 128:640], csl[:, 128:640],
        first=True, last=True,
        waits=[("scalar", qh2), ("vector", ql2)] + mk.attn_ps_guard.get(5, []), inc=1)
    a_tok = P.tok("tensor")
    af_ = wwork[:, 1024:1536]
    ev = P.do("scalar", lambda e, o=af_, i=ps_a: e.activation(o, i, AF.Copy, scale=SC),
              waits=[("tensor", a_tok)], inc=1)
    P.drain("scalar")
    ah = P.do("scalar", lambda e, o=whi[:, 512:1024], i=af_: e.activation(o, i, AF.Copy),
              inc=1)
    al = P.do(V, lambda e, o=wlo[:, 512:1024], a=af_, b=whi[:, 512:1024]:
              e.tensor_tensor(o, a, b, ALU.subtract), waits=[("scalar", ah)], inc=1)
    mk.attn_ps_guard[5] = [("scalar", ah)]
    tok3 = 0
    for lc in range(4):
        tok3 = P.do("sync", lambda e, o=wwork[:, lc * 128:(lc + 1) * 128],
                    i=dtb["wuv"][hp][lc * 128:(lc + 1) * 128, :]: e.dma_start(out=o, in_=i),
                    waits=[(wld, wld.cnt), ("scalar", hi_t), ("vector", lo_t)],
                    inc=16, sem=wld)
    uh = P.do("scalar", lambda e, o=whi[:, 1024:1536], i=wwork[:, 0:512]:
              e.activation(o, i, AF.Copy), waits=[(wld, tok3)], inc=1)
    ul = P.do(V, lambda e, o=wlo[:, 1024:1536], a=wwork[:, 0:512], b=whi[:, 1024:1536]:
              e.tensor_tensor(o, a, b, ALU.subtract), waits=[("scalar", uh)], inc=1)
    tok4 = P.do("sync", lambda e, o=wwork[:, 512:640], i=dtb["woh"][hp]:
                e.dma_start(out=o, in_=i),
                waits=[(wld, wld.cnt), ("scalar", qh2), ("vector", ql2)],
                inc=16, sem=wld)
    ps = mk.ps[4]
    tk = P.do("tensor", lambda e, o=ps[:, 0:128], i=wwork[:, 512:640], idn=mk.ident:
              e.transpose(o, i, idn), waits=[(wld, tok4)] + mk.attn_ps_guard.get(4, []), inc=1)
    oh_h, oh_l = split_from_psum(ps, whi[:, 1536:1664], wlo[:, 1536:1664], tk)
    mk.attn_ps_guard[4] = [("scalar", oh_h), ("vector", oh_l)]
    lnr = wwork[:, 512:1024]
    lbr = cwork[:, 1536:2048]
    P.do("sync", lambda e, o=lnr, i=dtb["lngr"][hp]: e.dma_start(out=o, in_=i),
         waits=[(wld, wld.cnt), ("scalar", oh_h), ("vector", oh_l)], inc=16, sem=wld)
    t6 = P.do("sync", lambda e, o=lbr, i=dtb["lnbr"][hp]: e.dma_start(out=o, in_=i),
              waits=[(wld, wld.cnt)], inc=16, sem=wld)

    d2d = P.xsem(f"d2d{hp}")
    for hl in range(2):
        P.do("gpsimd", lambda e, ll=hl, hh=hp:
             e.dma_start(out=dtb["xh_dram"][ll],
                         in_=dtb["ag1_out"][:, ll][:, bass.ds(mk.rv2[hh], 1)]
                         .rearrange("j o p t -> (j o) p t")),
             waits=[(cc, cc.cnt), (d2d, d2d.cnt)] + mk.attn_xh_guard, inc=16, sem=d2d)
    xhld = P.xsem(f"xhld{hp}")
    xh_hi, xh_lo = R["xh_hi"], R["xh_lo"]
    for j in range(NCO):
        for hl, dst in ((0, xh_hi), (1, xh_lo)):
            P.do("sync", lambda e, o=dst[:, j * T:(j + 1) * T], jj=j, ll=hl:
                 e.dma_start(out=o, in_=dtb["xh_dram"][ll, jj]),
                 waits=[(d2d, d2d.cnt)] + mk.attn_xh_guard, inc=16, sem=xhld)
    xh_tok = xhld.cnt
    mk.last_xhld = (xhld, xh_tok)

    vbuf = R["vbuf"]
    cbw = P.xsem(f"cbw{hp}")
    cguard = {0: [], 1: []}
    tr_guard = {2: [], 3: []}
    vg_guard = {6: [], 7: []}
    cn_rd = {0: 0, 1: 0}
    cn_rd_t = {0: 0, 1: 0}
    nst = min(4, NKT)
    stage_dma = [0]
    cprev = None

    def c_flush(base_kt):
        for lc in range(LCH):
            for hl, stg in ((0, csh), (1, csl)):
                stage_dma[0] = P.do("gpsimd", lambda e,
                                    o=dtb["cbuf"][hl, lc][:, base_kt * 128:(base_kt + nst) * 128],
                                    i=stg[:, lc * nst * 128:(lc + 1) * nst * 128]:
                                    e.dma_start(out=o, in_=i),
                                    waits=tr_guard[2] + tr_guard[3]
                                    + getattr(mk, "attn_prev_pan", []),
                                    inc=16, sem=cbw)

    def c_stageB(kt):
        par = kt % 2
        cn3 = cwork[:, par * 512:(par + 1) * 512]
        bank = 2 + par
        ps_t = mk.ps[bank]
        stcol = (kt % nst) * 128
        tk = 0
        for lc in range(LCH):
            tk = P.do("tensor", lambda e, o=ps_t[:, lc * 128:(lc + 1) * 128],
                      a=cn3[:, lc * 128:(lc + 1) * 128], b=mk.ident,
                      s=(lc == 0), q=(lc == LCH - 1):
                      e.matmul(o, a, b, is_transpose=True, start=s, stop=q),
                      waits=(([("vector", cn_rd[par])] + tr_guard.get(bank, []))
                             if lc == 0 else []), inc=(1 if lc == LCH - 1 else 0))
        cn_rd_t[par] = tk
        his = 0
        for lc in range(LCH):
            his = P.do("scalar", lambda e,
                       o=csh[:, lc * nst * 128 + stcol: lc * nst * 128 + stcol + 128],
                       i=ps_t[:, lc * 128:(lc + 1) * 128]: e.activation(o, i, AF.Copy),
                       waits=([("tensor", tk), (cbw, stage_dma[0])] if lc == 0 else []),
                       inc=1)
        los = 0
        for lc in range(LCH):
            los = P.do(V, lambda e,
                       o=csl[:, lc * nst * 128 + stcol: lc * nst * 128 + stcol + 128],
                       a=ps_t[:, lc * 128:(lc + 1) * 128],
                       b=csh[:, lc * nst * 128 + stcol: lc * nst * 128 + stcol + 128]:
                       e.tensor_tensor(o, a, b, ALU.subtract),
                       waits=[("scalar", his), (cbw, stage_dma[0])], inc=1)
        tr_guard[bank] = [("vector", los), ("scalar", his)]
        ps_v = mk.ps[6 + par]
        for lc in range(LCH):
            mm3(P, ps_v[:, 0:128], csh[:, lc * nst * 128 + stcol: lc * nst * 128 + stcol + 128],
                csl[:, lc * nst * 128 + stcol: lc * nst * 128 + stcol + 128],
                whi[:, 1024 + lc * 128:1152 + lc * 128], wlo[:, 1024 + lc * 128:1152 + lc * 128],
                first=(lc == 0), last=(lc == LCH - 1),
                waits=(([("scalar", his), ("vector", los)] + vg_guard.get(6 + par, []))
                       if lc == 0 else []), inc=(1 if lc == LCH - 1 else 0))
        vtk2 = P.tok("tensor")
        ev = P.do(V, lambda e, o=vbuf[:, kt * 128:(kt + 1) * 128], i=ps_v[:, 0:128]:
                  e.tensor_copy(o, i), waits=[("tensor", vtk2)], inc=1)
        vg_guard[6 + par] = [("vector", ev)]

    for kt in range(NKT):
        par = kt % 2
        ps_c = mk.ps[par]
        w0 = ([(xhld, xh_tok), (wld, t6), ("scalar", oh_h), ("vector", oh_l),
               ("vector", ul), ("vector", al)] if kt == 0 else [])
        mm3(P, ps_c, xh_hi[:, kt * 128:(kt + 1) * 128], xh_lo[:, kt * 128:(kt + 1) * 128],
            whi[:, 0:512], wlo[:, 0:512], first=True, last=True,
            waits=w0 + cguard.get(par, []), inc=1)
        ctk = P.tok("tensor")
        sb0 = 14 * par
        P.do(V, lambda e, o=astat[:, sb0:sb0 + 6], i=ps_c: e.bn_stats(o, i),
             waits=[("tensor", ctk)])
        P.drain(V)
        P.do(V, lambda e, o=astat[:, sb0 + 6:sb0 + 8], i=astat[:, sb0:sb0 + 6]:
             e.bn_aggr(o, i))
        P.drain(V)
        P.do(V, lambda e, o=astat[:, sb0 + 8:sb0 + 9], i=astat[:, sb0 + 7:sb0 + 8]:
             e.tensor_scalar(o, i, 1e-5, None, ALU.add))
        P.drain(V, inc=1)
        vtk = P.tok(V)
        stk = P.do("scalar", lambda e, o=astat[:, sb0 + 9:sb0 + 10],
                   i=astat[:, sb0 + 8:sb0 + 9]:
                   e.activation(o, i, AF.Sqrt), waits=[("vector", vtk)], inc=1)
        P.do(V, lambda e, o=astat[:, sb0 + 10:sb0 + 11], i=astat[:, sb0 + 9:sb0 + 10]:
             e.reciprocal(o, i), waits=[("scalar", stk)])
        P.drain(V)
        P.do(V, lambda e, o=astat[:, sb0 + 11:sb0 + 12], a=astat[:, sb0 + 6:sb0 + 7],
             b=astat[:, sb0 + 10:sb0 + 11]:
             e.scalar_tensor_tensor(o, a, -1.0, b, ALU.mult, ALU.mult))
        P.drain(V, inc=1)
        st_tok = P.tok(V)
        if cprev is not None:
            c_stageB(cprev)
            if (cprev + 1) % nst == 0:
                c_flush(cprev + 1 - nst)
        cn3 = cwork[:, par * 512:(par + 1) * 512]
        ap_tok = P.do("scalar", lambda e, o=cn3, i=ps_c, sc=astat[:, sb0 + 10:sb0 + 11],
                      bi=astat[:, sb0 + 11:sb0 + 12]:
                      e.activation(o, i, AF.Identity, scale=sc, bias=bi),
                      waits=[("vector", st_tok), ("tensor", cn_rd_t.get(par, 0))], inc=1)
        cguard[par] = [("scalar", ap_tok)]
        P.do(V, lambda e, o=cn3, a=cn3, b=lnr: e.tensor_tensor(o, a, b, ALU.mult),
             waits=[("scalar", ap_tok)])
        P.drain(V)
        P.do(V, lambda e, o=cn3, a=cn3, b=lbr: e.tensor_tensor(o, a, b, ALU.add))
        P.drain(V, inc=1)
        cn_rd[par] = P.tok(V)
        cprev = kt
    if cprev is not None:
        c_stageB(cprev)
        c_flush(cprev + 1 - nst)
    cb_done = cbw.cnt
    mk.attn_sb_dma_reads.append((cbw, cbw.cnt))
    vb_done = P.drain(V, inc=1)

    qt_hi, qt_lo = R["qt_hi"], R["qt_lo"]
    span, epan, ohacc = R["span"], R["epan"], R["ohacc"]
    o2st = R["o2st"]
    cpan = R["cpan"]
    cpsem = [P.xsem(f"cp{hp}a"), P.xsem(f"cp{hp}b")]
    cpan_guard = [[], []]
    sc_guard = {0: [], 1: []}
    et_guard = {2: [], 3: []}
    oh_ps_guard = {4: [], 5: []}
    qtg = mk.attn_qt_guard
    prev_exp = {0: 0, 1: 0}
    prev_epan_rd = {0: 0, 1: 0}
    eT_free = {0: [], 1: []}
    prev_ag2 = ag2w.cnt
    span2, epan2 = R["span2"], R["epan2"]
    fsS = R["fsS"]
    mk.cstage_sfree = getattr(mk, "cstage_sfree", {})
    GW = GQT * 128
    for g in range(NGRP):
        q0 = g * GW
        qsp_tok = 0
        for lc in range(LCH):
            for qs in range(GW // 512):
                ps_q = mk.ps[qs % 2]
                mm3(P, ps_q, whi[:, 512 + lc * 128:640 + lc * 128],
                    wlo[:, 512 + lc * 128:640 + lc * 128],
                    xh_hi[:, q0 + qs * 512: q0 + (qs + 1) * 512],
                    xh_lo[:, q0 + qs * 512: q0 + (qs + 1) * 512],
                    first=True, last=True,
                    waits=sc_guard.get(qs % 2, []) + qtg + [("vector", vb_done)], inc=1)
                qtk = P.tok("tensor")
                col = lc * GW + qs * 512
                hq = P.do("scalar", lambda e, o=qt_hi[:, col:col + 512], i=ps_q:
                          e.activation(o, i, AF.Copy), waits=[("tensor", qtk)], inc=1)
                lq = P.do(V, lambda e, o=qt_lo[:, col:col + 512], a=ps_q, b=qt_hi[:, col:col + 512]:
                          e.tensor_tensor(o, a, b, ALU.subtract), waits=[("scalar", hq)], inc=1)
                sc_guard[qs % 2] = [("scalar", hq), ("vector", lq)]
        qt_done = [("scalar", P.tok("scalar")), ("vector", P.tok("vector"))]
        P.do(V, lambda e, o=astat[:, 16:16 + GQT]: e.memset(o, -1e30), waits=qt_done)
        P.do(V, lambda e, o=astat[:, 48:48 + GQT]: e.memset(o, 0.0))
        iz = P.do(V, lambda e, o=ohacc: e.memset(o, 0.0),
                  waits=[("tensor", P.tok("tensor"))], inc=1)
        P.drain(V, inc=1)
        mslot = {j: 0 for j in range(GQT)}

        def av_stage(pv, iz=iz):
            ii, p, j, kext = pv["ii"], pv["p"], pv["j"], pv["kext"]
            par = pv["par"]
            sc0 = 8 * par
            epv = epan if par == 0 else epan2
            ps_o = mk.ps[4 + ii % 2]
            nkb = (kext + 127) // 128
            groups = [(gg * 4, min(4, nkb - gg * 4)) for gg in range((nkb + 3) // 4)]
            evts = {}

            def emit_T(gi):
                kb0, n4 = groups[gi]
                bank = 2 + gi % 2
                ps_t = mk.ps[bank]
                tk = 0
                for u in range(n4):
                    kb = kb0 + u
                    w0 = ([("scalar", pv["ex"])] + et_guard.get(bank, [])) if u == 0 else []
                    tk = P.do("tensor", lambda e, o=ps_t[:, u * 128:(u + 1) * 128],
                              a=epv[:, kb * 128:(kb + 1) * 128], b=mk.ident,
                              s=(u == 0), q=(u == n4 - 1):
                              e.matmul(o, a, b, is_transpose=True, start=s, stop=q),
                              waits=w0, inc=(1 if u == n4 - 1 else 0))
                eng = "vector" if gi % 2 == 0 else "scalar"
                slot = cwork[:, (gi % 2) * 512:(gi % 2) * 512 + 512]
                if eng == "vector":
                    evt = P.do(V, lambda e, o=slot[:, 0:n4 * 128], i=ps_t[:, 0:n4 * 128]:
                               e.tensor_copy(o, i),
                               waits=[("tensor", tk)] + eT_free[gi % 2], inc=1)
                else:
                    evt = P.do("scalar", lambda e, o=slot[:, 0:n4 * 128], i=ps_t[:, 0:n4 * 128]:
                               e.activation(o, i, AF.Copy),
                               waits=[("tensor", tk)] + eT_free[gi % 2], inc=1)
                et_guard[bank] = [(eng, evt)]
                evts[gi] = (eng, evt)

            def emit_AV(gi):
                kb0, n4 = groups[gi]
                slot = cwork[:, (gi % 2) * 512:(gi % 2) * 512 + 512]
                tk = 0
                for u in range(n4):
                    kb = kb0 + u
                    ktile = 8 * p + kb
                    w0 = [evts[gi]] if u == 0 else []
                    if kb == 0:
                        w0 = w0 + oh_ps_guard.get(4 + ii % 2, [])
                    tk = P.do("tensor", lambda e, o=ps_o[:, 0:128],
                              a=slot[:, u * 128:(u + 1) * 128],
                              b=vbuf[:, ktile * 128:(ktile + 1) * 128],
                              s=(kb == 0), q=(kb == nkb - 1):
                              e.matmul(o, a, b, start=s, stop=q),
                              waits=w0, inc=(1 if u == n4 - 1 else 0))
                eT_free[gi % 2] = [("tensor", tk)]

            for gi in range(len(groups)):
                emit_T(gi)
                if gi > 0:
                    emit_AV(gi - 1)
            emit_AV(len(groups) - 1)
            av_tok = P.tok("tensor")
            prev_epan_rd[par] = av_tok
            pv["av"] = av_tok

        def av_stage2(pv, iz=iz):
            ii, j = pv["ii"], pv["j"]
            par = pv["par"]
            sc0 = 8 * par
            ps_o = mk.ps[4 + ii % 2]
            mg = P.do(V, lambda e, o=ohacc[:, j * 128:(j + 1) * 128],
                      i=ohacc[:, j * 128:(j + 1) * 128], a=astat[:, sc0 + 2:sc0 + 3],
                      b=ps_o[:, 0:128]:
                      e.scalar_tensor_tensor(o, i, a, b, ALU.mult, ALU.add),
                      waits=[("tensor", pv["av"]), ("scalar", pv["al"]), ("vector", iz)],
                      inc=1)
            sj = 48 + j
            P.do(V, lambda e, o=astat[:, sj:sj + 1], i=astat[:, sj:sj + 1],
                 a=astat[:, sc0 + 2:sc0 + 3], b=astat[:, sc0 + 4:sc0 + 5]:
                 e.scalar_tensor_tensor(o, i, a, b, ALU.mult, ALU.add),
                 waits=[("scalar", pv["ex"])])
            P.drain(V, inc=1)
            oh_ps_guard[4 + ii % 2] = [("vector", P.tok(V))]

        npan = (q0 + GW) // 1024
        its = []
        for p in range(npan):
            for j in range(GQT):
                jt = g * GQT + j
                kext = min(1024, 128 * (jt + 1) - 1024 * p)
                if kext > 0:
                    its.append((p, j, jt, kext))

        def load_panel(pp):
            sl2 = pp % 2
            cg = cpan_guard[sl2] + [("tensor", P.tok("tensor"))]
            for lc in range(LCH):
                for hl, dstp in ((0, cpan[sl2]["hi"]), (1, cpan[sl2]["lo"])):
                    P.do("sync", lambda e, o=dstp[:, lc * 1024:(lc + 1) * 1024],
                         i=dtb["cbuf"][hl, lc][:, pp * 1024:(pp + 1) * 1024]:
                         e.dma_start(out=o, in_=i),
                         waits=[(cbw, cb_done)] + cg, inc=16, sem=cpsem[sl2])
            pan_tok[pp] = cpsem[sl2].cnt

        pan_tok = {}
        load_panel(0)
        cur_p = -1
        prev = None
        for ii, (p, j, jt, kext) in enumerate(its):
            if p != cur_p:
                cur_p = p
                if p + 1 < npan:
                    load_panel(p + 1)
            sl = p % 2
            par = ii % 2
            sc0 = 8 * par
            spv = span if par == 0 else span2
            epv = epan if par == 0 else epan2
            nkh = (kext + 511) // 512
            khw = [min(512, kext - ki * 512) for ki in range(nkh)]
            sc_w = [(cpsem[sl], pan_tok[p])] + sc_guard.get(0, []) + sc_guard.get(1, [])
            for lc in range(LCH):
                qh = qt_hi[:, lc * GW + j * 128:lc * GW + (j + 1) * 128]
                ql = qt_lo[:, lc * GW + j * 128:lc * GW + (j + 1) * 128]
                for ki in range(nkh):
                    kh = ki * 512
                    w = khw[ki]
                    P.do("tensor", lambda e, o=mk.ps[ki][:, 0:w],
                         a=qh, b=cpan[sl]["hi"][:, lc * 1024 + kh:lc * 1024 + kh + w],
                         s=(lc == 0): e.matmul(o, a, b, start=s, stop=False),
                         waits=sc_w)
                    sc_w = ()
                    P.do("tensor", lambda e, o=mk.ps[ki][:, 0:w],
                         a=qh, b=cpan[sl]["lo"][:, lc * 1024 + kh:lc * 1024 + kh + w]:
                         e.matmul(o, a, b, start=False, stop=False))
                for ki in range(nkh):
                    kh = ki * 512
                    w = khw[ki]
                    P.do("tensor", lambda e, o=mk.ps[ki][:, 0:w],
                         a=ql, b=cpan[sl]["hi"][:, lc * 1024 + kh:lc * 1024 + kh + w],
                         s=(lc == LCH - 1): e.matmul(o, a, b, start=False, stop=s),
                         inc=(1 if (lc == LCH - 1 and ki == nkh - 1) else 0))
            stk_ = P.tok("tensor")
            if prev is not None:
                av_stage(prev)
            for ki in range(nkh):
                kh = ki * 512
                w = khw[ki]
                mc = astat[:, sc0:sc0 + 1]
                if ki == 0:
                    ev = P.do(V, lambda e, o=spv[:, kh:kh + w], i=mk.ps[ki][:, 0:w], m=mc:
                              e.tensor_scalar(o, i, 0.0, None, ALU.add, ALU.max, accum_out=m),
                              waits=[("tensor", stk_), ("scalar", prev_exp.get(par, 0))], inc=1)
                else:
                    P.drain(V)
                    ev = P.do(V, lambda e, o=spv[:, kh:kh + w], i=mk.ps[ki][:, 0:w], m=mc:
                              e.tensor_scalar(o, i, 0.0, m, ALU.add, ALU.max, accum_out=m),
                              inc=1)
                sc_guard[ki] = [("vector", ev)]
            if 128 * (jt + 1) <= 1024 * (p + 1):
                P.drain(V)
                dcol = 128 * jt - 1024 * p
                P.do(V, lambda e, o=spv[:, dcol:dcol + 128], a=spv[:, dcol:dcol + 128],
                     b=mk.tril: e.tensor_tensor(o, a, b, ALU.add))
            P.drain(V)
            mo = (16 + j if mslot[j] == 0 else 32 + j)
            mn = (32 + j if mslot[j] == 0 else 16 + j)
            P.do(V, lambda e, o=astat[:, mn:mn + 1], a=astat[:, mo:mo + 1],
                 b=astat[:, sc0:sc0 + 1]: e.tensor_tensor(o, a, b, ALU.max))
            P.drain(V)
            P.do(V, lambda e, o=astat[:, sc0 + 2:sc0 + 3], a=astat[:, mo:mo + 1],
                 b=astat[:, mn:mn + 1]: e.tensor_tensor(o, a, b, ALU.subtract))
            P.do(V, lambda e, o=astat[:, sc0 + 3:sc0 + 4], i=astat[:, mn:mn + 1]:
                 e.tensor_scalar(o, i, -1.0, None, ALU.mult))
            P.drain(V, inc=1)
            mtok = P.tok(V)
            al_tok = P.do("scalar", lambda e, o=astat[:, sc0 + 2:sc0 + 3],
                          i=astat[:, sc0 + 2:sc0 + 3]:
                          e.activation(o, i, AF.Exp), waits=[("vector", mtok)], inc=1)
            ex_tok = P.do("scalar", lambda e, o=epv[:, 0:kext], i=spv[:, 0:kext],
                          b=astat[:, sc0 + 3:sc0 + 4], s=astat[:, sc0 + 4:sc0 + 5]:
                          e.activation(o, i, AF.Exp, bias=b, accum_out=s),
                          waits=[("vector", mtok), ("tensor", prev_epan_rd.get(par, 0))],
                          inc=1)
            prev_exp[par] = ex_tok
            if prev is not None:
                av_stage2(prev)
            prev = dict(ii=ii, p=p, j=j, kext=kext, ex=ex_tok, al=al_tok, par=par)
            mslot[j] ^= 1
        if prev is not None:
            av_stage(prev)
            av_stage2(prev)
        oh_done = P.drain(V, inc=1)
        P.do(V, lambda e, o=astat[:, 0:GQT], i=astat[:, 48:48 + GQT]: e.reciprocal(o, i),
             waits=[("vector", oh_done)])
        rc_tok = P.drain(V, inc=1)
        ohT_hi = csh
        ohT_lo = csl
        spl_tok = []
        for gi in range((GQT + 3) // 4):
            n4 = min(4, GQT - gi * 4)
            bank = 2 + gi % 2
            ps_t = mk.ps[bank]
            tk = 0
            for u in range(n4):
                j = gi * 4 + u
                tk = P.do("tensor", lambda e, o=ps_t[:, u * 128:(u + 1) * 128],
                          a=ohacc[:, j * 128:(j + 1) * 128], b=mk.ident,
                          s=(u == 0), q=(u == n4 - 1):
                          e.matmul(o, a, b, is_transpose=True, start=s, stop=q),
                          waits=([("vector", oh_done)] + et_guard.get(bank, []))
                          if u == 0 else [], inc=(1 if u == n4 - 1 else 0))
            hs = P.do("scalar", lambda e, o=ohT_hi[:, gi * 512:gi * 512 + n4 * 128],
                      i=ps_t[:, 0:n4 * 128]: e.activation(o, i, AF.Copy),
                      waits=[("tensor", tk), ("scalar", mk.cstage_sfree.get(hp, 0))], inc=1)
            ls = P.do(V, lambda e, o=ohT_lo[:, gi * 512:gi * 512 + n4 * 128],
                      a=ps_t[:, 0:n4 * 128], b=ohT_hi[:, gi * 512:gi * 512 + n4 * 128]:
                      e.tensor_tensor(o, a, b, ALU.subtract),
                      waits=[("scalar", hs)], inc=1)
            et_guard[bank] = [("vector", ls), ("scalar", hs)]
            spl_tok.append((hs, ls))
        hbuf = fsS
        hb_tok = 0
        for j in range(GQT):
            gi = j // 4
            hs, ls = spl_tok[gi]
            ps_2 = mk.ps[6 + j % 2]
            mm3(P, ps_2[:, 0:128], ohT_hi[:, j * 128:(j + 1) * 128],
                ohT_lo[:, j * 128:(j + 1) * 128],
                whi[:, 1536:1664], wlo[:, 1536:1664], first=True, last=True,
                waits=[("scalar", hs), ("vector", ls)] + mk.attn_ps_guard.get(6 + j % 2, []),
                inc=1)
            o2_tok = P.tok("tensor")
            hb_tok = P.do(V, lambda e, o=hbuf[:, j * 128:(j + 1) * 128], i=ps_2[:, 0:128],
                          s=astat[:, j:j + 1]:
                          e.tensor_scalar(o, i, s, None, ALU.mult),
                          waits=[("tensor", o2_tok), ("vector", rc_tok)], inc=1)
            mk.attn_ps_guard[6 + j % 2] = [("vector", hb_tok)]
        hb_done = P.drain(V, inc=1)
        HG = min(8, GQT)
        for gi in range((GQT + 3) // 4):
            n4 = min(4, GQT - gi * 4)
            bank = 2 + gi % 2
            ps_t = mk.ps[bank]
            tk = 0
            for u in range(n4):
                j = gi * 4 + u
                tk = P.do("tensor", lambda e, o=ps_t[:, u * 128:(u + 1) * 128],
                          a=hbuf[:, j * 128:(j + 1) * 128], b=mk.ident,
                          s=(u == 0), q=(u == n4 - 1):
                          e.matmul(o, a, b, is_transpose=True, start=s, stop=q),
                          waits=([("vector", hb_done)] + et_guard.get(bank, []))
                          if u == 0 else [], inc=(1 if u == n4 - 1 else 0))
            sc = (gi * 512) % (HG * 128)
            f2h = P.do("scalar", lambda e, o=o2st[:, sc:sc + n4 * 128],
                       i=ps_t[:, 0:n4 * 128]: e.activation(o, i, AF.Copy),
                       waits=[("tensor", tk), (ag2w, prev_ag2)], inc=1)
            f2l = P.do(V, lambda e, o=o2st[:, HG * 128 + sc:HG * 128 + sc + n4 * 128],
                       a=ps_t[:, 0:n4 * 128], b=o2st[:, sc:sc + n4 * 128]:
                       e.tensor_tensor(o, a, b, ALU.subtract),
                       waits=[("scalar", f2h)], inc=1)
            et_guard[bank] = [("vector", f2l), ("scalar", f2h)]
            if (gi + 1) % 2 == 0:
                qb = q0 + (gi - 1) * 512
                stg_tok2 = [("scalar", f2h), ("vector", f2l)]
                for hl in range(2):
                    P.do("gpsimd", lambda e, o=dtb["o2buf"][hp, hl][:, qb:qb + HG * 128],
                         i=o2st[:, hl * HG * 128:(hl + 1) * HG * 128]:
                         e.dma_start(out=o, in_=i),
                         waits=stg_tok2, inc=16, sem=ag2w)
                prev_ag2 = ag2w.cnt
        mk.cstage_sfree[hp] = P.tok("scalar")
    done = [(eng, P.tok(eng)) for eng in ("tensor", "vector", "scalar")]
    mk.attn_head_guard[hp + 1] = done
    mk.attn_xh_guard = [("tensor", P.tok("tensor")), mk.last_xhld]
    mk.attn_qt_guard = [("tensor", P.tok("tensor"))]
    mk.attn_prev_pan = [(cpsem[0], cpsem[0].cnt), (cpsem[1], cpsem[1].cnt)]


def _emit_wo_partial(mk, R, o2w):
    P, cfg = mk.P, mk.cfg
    T = cfg["T"]; D = cfg["D"]; S = cfg["S"]; NCO = cfg["n_cores"]
    V = "vector"
    dtb = mk.dt
    o2sl = R["cstage"]
    wot_hi = R["cpan"][0]["hi"]
    wot_lo = R["cpan"][0]["lo"]
    wot_hi2 = R["cpan"][1]["hi"]
    wot_lo2 = R["cpan"][1]["lo"]
    wraw = R["wwork"]
    stg = [R["span"], R["epan"]]
    done = mk.attn_head_guard[2]
    wsem = P.xsem("wops")
    sp_tok = []
    for hpi, (wh, wl) in enumerate(((wot_hi, wot_lo), (wot_hi2, wot_lo2))):
        for pc in range(D // 512):
            dtok = P.do("sync", lambda e, o=wraw[:, 0:512] if pc % 2 == 0 else wraw[:, 512:1024],
                        i=dtb["woT"][hpi][:, pc * 512:(pc + 1) * 512]:
                        e.dma_start(out=o, in_=i),
                        waits=done + (sp_tok[-2:] if len(sp_tok) >= 2 else []), inc=16, sem=wsem)
            rr = wraw[:, 0:512] if pc % 2 == 0 else wraw[:, 512:1024]
            ht = P.do("scalar", lambda e, o=wh[:, pc * 512:(pc + 1) * 512], i=rr:
                      e.activation(o, i, AF.Copy), waits=[(wsem, dtok)] + done, inc=1)
            lt = P.do(V, lambda e, o=wl[:, pc * 512:(pc + 1) * 512], a=rr,
                      b=wh[:, pc * 512:(pc + 1) * 512]:
                      e.tensor_tensor(o, a, b, ALU.subtract), waits=[("scalar", ht)], inc=1)
            sp_tok.append(("scalar", ht))
            sp_tok.append(("vector", lt))
    w_ready = [("scalar", P.tok("scalar")), ("vector", P.tok("vector"))]
    osem = [P.xsem("o2a"), P.xsem("o2b")]
    o_guard = [[], []]
    bank_free = {i: [] for i in range(8)}
    rsems = [P.xsem(f"rs{i}") for i in range(4)]
    mk.rsw = rsems[0]
    stg_guard = {i: [] for i in range(4)}
    setsel = 0
    for tt in range(S // 512):
        sl = tt % 2
        otok = 0
        for hpi in range(2):
            for hl in range(2):
                off = sl * 2048 + hpi * 1024 + hl * 512
                otok = P.do("sync", lambda e, o=o2sl[:, off:off + 512],
                            i=dtb["o2buf"][hpi, hl][:, tt * 512:(tt + 1) * 512]:
                            e.dma_start(out=o, in_=i),
                            waits=[(o2w, o2w.cnt)] + o_guard[sl] + done, inc=16, sem=osem[sl])
        for sub in range(4):
            bs = setsel * 4
            banks = {eo: mk.ps[bs + eo] for eo in range(D // 512)}
            for eo in range(D // 512):
                mmw = [(osem[sl], otok)] + w_ready + bank_free.get(bs + eo, []) + o_guard[sl]
                for hpi in range(2):
                    wh = (wot_hi, wot_hi2)[hpi]
                    wl = (wot_lo, wot_lo2)[hpi]
                    off = sl * 2048 + hpi * 1024
                    mm3(P, banks[eo],
                        o2sl[:, off + sub * 128: off + (sub + 1) * 128],
                        o2sl[:, off + 512 + sub * 128: off + 512 + (sub + 1) * 128],
                        wh[:, eo * 512:(eo + 1) * 512], wl[:, eo * 512:(eo + 1) * 512],
                        first=(hpi == 0), last=(hpi == 1), waits=mmw,
                        inc=(1 if hpi == 1 else 0))
                    mmw = ()
                t_tok = P.tok("tensor")
                si = (sub % 2) * 2 + (eo % 2)
                sgb = stg[sub % 2]
                col = (0, 512)[eo % 2]
                ev = P.do(V, lambda e, o=sgb[:, col:col + 512],
                          i=banks[eo]: e.tensor_copy(o, i),
                          waits=[("tensor", t_tok)] + stg_guard[si], inc=1)
                bank_free[bs + eo] = [("vector", ev)]
                trow = tt * 512 + sub * 128
                EOH = (D // 512 + cfg["NRSH"] - 1) // cfg["NRSH"]
                dtk = P.do("gpsimd", lambda e,
                           o=dtb["rs_in"][eo // EOH][trow:trow + 128,
                                                     (eo % EOH) * 512:(eo % EOH + 1) * 512],
                           i=sgb[:, col:col + 512]:
                           e.dma_start(out=o, in_=i), waits=[("vector", ev)], inc=16,
                           sem=rsems[si])
                stg_guard[si] = [(rsems[si], dtk)]
            setsel ^= 1
        o_guard[sl] = [("tensor", P.tok("tensor"))]
    mk.rs_done = [(s, s.cnt) for s in rsems]


def emit_wo_final(mk):
    P, cfg = mk.P, mk.cfg
    T = cfg["T"]; D = cfg["D"]; NCH = D // 128
    V = "vector"
    dtb = mk.dt
    af32, abf = mk.af32, mk.abf
    af32.push()
    tmp = af32.take(2 * D, "wof_tm")
    bo_sb = af32.take(NCH, "bo")
    ad = mk.attn_done
    xld = P.xsem("xld")
    for c in range(NCH):
        P.do("sync", lambda e, o=mk.xres[:, c * T:(c + 1) * T], i=dtb["xspill"][c]:
             e.dma_start(out=o, in_=i),
             waits=[(mk.spx, mk.spx.cnt), ("vector", ad["v"]), ("scalar", ad["s"]),
                    ("tensor", ad["t"])] + mk.rs_done, inc=16, sem=xld)
    bld = P.xsem("bld")
    P.do("sync", lambda e, o=bo_sb[:, 0:NCH],
         i=dtb["bo"].rearrange("(c p) -> p c", p=128):
         e.dma_start(out=o, in_=i, allow_slow_non_contiguous=True),
         waits=[("vector", ad["v"]), ("tensor", ad["t"])] + mk.rs_done,
         inc=16, sem=bld)
    rsl = [P.xsem("rsl0"), P.xsem("rsl1")]
    bank_g = {0: [], 1: []}
    slot_g = {0: [], 1: []}
    NRSH = cfg["NRSH"]
    for hf in range(NRSH):
        cc_need = hf + 1
        for q in range(T // 128):
            sl = q % 2
            tv = tmp[:, sl * D: sl * D + D // NRSH]
            dtok = P.do("sync", lambda e, o=tv,
                        i=dtb["rs_out"][hf][q * 128:(q + 1) * 128, :]:
                        e.dma_start(out=o, in_=i),
                        waits=[(mk.cc2, cc_need), ("vector", ad["v"])] + mk.rs_done
                        + slot_g.get(sl, []),
                        inc=16, sem=rsl[sl])
            tk = 0
            for ci in range(NCH // NRSH):
                c = hf * (NCH // NRSH) + ci
                ps = mk.ps[ci % 2]
                tk = P.do("tensor", lambda e, o=ps[:, 0:128], i=tv[:, ci * 128:(ci + 1) * 128],
                          idn=mk.ident: e.transpose(o, i, idn),
                          waits=[(rsl[sl], dtok), ("tensor", ad["t"])] + bank_g.get(ci % 2, []),
                          inc=1)
                ev = P.do(V, lambda e, o=mk.xres[:, c * T + q * 128:c * T + (q + 1) * 128],
                          i=ps[:, 0:128], s=bo_sb[:, c:c + 1],
                          a=mk.xres[:, c * T + q * 128:c * T + (q + 1) * 128]:
                          e.scalar_tensor_tensor(o, i, s, a, ALU.add, ALU.add),
                          waits=[("tensor", tk), (xld, xld.cnt), (bld, bld.cnt)], inc=1)
                bank_g[ci % 2] = [("vector", ev)]
            slot_g[sl] = [("tensor", tk)]
    P.drain(V, inc=1)
    mk.x_ready = {"v": P.tok("vector"), "t": P.tok("tensor"), "s": P.tok("scalar")}
    af32.pop()


def emit_store_y(mk):
    P, cfg = mk.P, mk.cfg
    T = cfg["T"]; D = cfg["D"]; NCH = D // 128
    nq = T // 128
    af32 = mk.af32
    af32.push()
    tmp = af32.take(2 * D, "ytm")
    xr = mk.x_ready
    sems = [P.xsem("ya"), P.xsem("yb")]
    prev_dma = [0, 0]
    ev_tok = {0: 0, 1: 0}
    for q in range(nq):
        slot = q % 2
        tmv = tmp[:, slot * D:(slot + 1) * D]
        last_ev = 0
        for c in range(NCH):
            ps = mk.ps[c % 2]
            tk = P.do("tensor", lambda e, o=ps[:, 0:128],
                      i=mk.xres[:, c * T + q * 128: c * T + (q + 1) * 128], idn=mk.ident:
                      e.transpose(o, i, idn),
                      waits=[("vector", xr["v"]), ("vector", ev_tok[c % 2])], inc=1)
            last_ev = P.do("vector", lambda e, o=tmv[:, c * 128:(c + 1) * 128], i=ps[:, 0:128]:
                           e.tensor_copy(o, i),
                           waits=[("tensor", tk), (sems[slot], prev_dma[slot])], inc=1)
            ev_tok[c % 2] = last_ev
        prev_dma[slot] = P.do("sync", lambda e, o=mk.dt["y"][q * 128:(q + 1) * 128, :], i=tmv:
                              e.dma_start(out=o, in_=i), waits=[("vector", last_ev)], inc=16,
                              sem=sems[slot])
    P.do("sync", lambda e: e.nop(),
         waits=[(sems[0], sems[0].cnt), (sems[1], sems[1].cnt)])
    af32.pop()


from concourse.bass_utils import run_bass_kernel_spmd
import ml_dtypes

_NCORES = 8
_S, _D = 8192, 2048
_T = _S // _NCORES
_CFG = dict(T=_T, D=_D, S=_S, DE=1408, DS=2816, NEXP=8,
            phases=("ffn1", "attn", "ffn2"), n_cores=_NCORES)

_NC_CACHE = {}


def _build_once():
    if "nc" not in _NC_CACHE:
        _NC_CACHE["nc"] = build(_CFG)
    return _NC_CACHE["nc"]


def kernel(L0_attn_x, L0_attn_Wq, L0_attn_Wdkv, L0_attn_Wuk, L0_attn_Wuv,
           L0_attn_ln_g, L0_attn_ln_b, L0_attn_Wo_h, L0_attn_Wo, L0_attn_bo,
           L0_ffn_W_router, L0_ffn_We_gate, L0_ffn_We_up, L0_ffn_We_down,
           L0_ffn_Ws_gate, L0_ffn_Ws_up, L0_ffn_Ws_down):
    nc, mk_obj = _build_once()
    x = np.ascontiguousarray(np.asarray(L0_attn_x, dtype=np.float32).reshape(_S, _D))
    wo = np.asarray(L0_attn_Wo, dtype=np.float32)
    consts = {
        "ident": np.eye(128, dtype=np.float32),
        "identb": np.eye(128, dtype=ml_dtypes.bfloat16),
        "tril": np.where(np.tril(np.ones((128, 128), bool)), 0.0, -1e30).astype(np.float32),
        "esel": np.stack([np.repeat(np.eye(8, dtype=np.float32)[:, e:e + 1], 128, axis=1)
                          for e in range(8)]),
    }
    shared = {
        "wr": np.asarray(L0_ffn_W_router, dtype=np.float32),
        "wsg": np.asarray(L0_ffn_Ws_gate, dtype=np.float32),
        "wsu": np.asarray(L0_ffn_Ws_up, dtype=np.float32),
        "wsd": np.asarray(L0_ffn_Ws_down, dtype=np.float32),
        "weg": np.asarray(L0_ffn_We_gate, dtype=np.float32),
        "weu": np.asarray(L0_ffn_We_up, dtype=np.float32),
        "wed": np.asarray(L0_ffn_We_down, dtype=np.float32),
        "wo": None,
        "bo": np.asarray(L0_attn_bo, dtype=np.float32),
    }
    del shared["wo"]
    wq = np.asarray(L0_attn_Wq, dtype=np.float32)
    wdkv = np.asarray(L0_attn_Wdkv, dtype=np.float32)
    wuk = np.asarray(L0_attn_Wuk, dtype=np.float32)
    wuv = np.asarray(L0_attn_Wuv, dtype=np.float32)
    woh = np.asarray(L0_attn_Wo_h, dtype=np.float32)
    lng = np.asarray(L0_attn_ln_g, dtype=np.float32)
    lnb = np.asarray(L0_attn_ln_b, dtype=np.float32)
    in_maps = []
    for r in range(_NCORES):
        hs = slice(2 * r, 2 * r + 2)
        ins = dict(consts)
        ins.update(shared)
        ins.update({
            "x0": np.ascontiguousarray(x[r * _T:(r + 1) * _T]),
            "wq": np.ascontiguousarray(wq[hs]),
            "wdkv": np.ascontiguousarray(wdkv[hs]),
            "wuk": np.ascontiguousarray(wuk[hs]),
            "wuv": np.ascontiguousarray(wuv[hs]),
            "woh": np.ascontiguousarray(woh[hs]),
            "lngr": np.ascontiguousarray(np.repeat(lng[hs][:, None, :], 128, axis=1)),
            "lnbr": np.ascontiguousarray(np.repeat(lnb[hs][:, None, :], 128, axis=1)),
            "woT": np.ascontiguousarray(
                wo[:, 256 * r:256 * r + 256].T.reshape(2, 128, _D)),
            "ridx": np.array([[r]], np.int32),
        })
        in_maps.append(ins)
    res = run_bass_kernel_spmd(nc, in_maps, core_ids=list(range(_NCORES)))
    y = np.concatenate([np.asarray(res.results[r]["y"]) for r in range(_NCORES)], axis=0)
    return y.reshape(1, _S, _D).astype(np.float32)


# version 08
# speedup vs baseline: 1.3711x; 1.0004x over previous
import numpy as np
from contextlib import ExitStack
import concourse.bass as bass
import concourse.mybir as mybir

F32 = mybir.dt.float32
BF16 = mybir.dt.bfloat16
AF = mybir.ActivationFunctionType
ALU = mybir.AluOpType
AX = mybir.AxisListType

ENGS = ("sync", "scalar", "vector", "tensor", "gpsimd")


class XSem:

    def __init__(self, handle, name):
        self.h = handle
        self.name = name
        self.cnt = 0


class Prog:

    def __init__(self, nc, stack):
        self.nc = nc
        self.stack = stack
        self.q = {e: [] for e in ENGS}
        self.sem = {e: stack.enter_context(nc.semaphore(f"c_{e}")) for e in ENGS}
        self.cnt = {e: 0 for e in ENGS}
        self.waited = {e: {} for e in ENGS}

    def xsem(self, name):
        self._xsem_n = getattr(self, "_xsem_n", 0) + 1
        uname = f"{name}_{self._xsem_n}"
        return XSem(self.stack.enter_context(self.nc.semaphore(uname)), uname)

    def _handle(self, key):
        return key.h if isinstance(key, XSem) else self.sem[key]

    def do(self, eng, fn, waits=(), inc=0, sem=None):
        for (k, v) in waits:
            kk = k.name if isinstance(k, XSem) else k
            if v > 0 and self.waited[eng].get(kk, 0) < v:
                self.waited[eng][kk] = v
                self.q[eng].append(("w", self._handle(k), v))
        if inc:
            if sem is not None:
                sem.cnt += inc
                self.q[eng].append(("i", fn, sem.h, inc))
                return sem.cnt
            self.cnt[eng] += inc
            self.q[eng].append(("i", fn, self.sem[eng], inc))
        else:
            assert sem is None
            self.q[eng].append(("o", fn))
        return self.cnt[eng]

    def drain(self, eng, inc=0):
        return self.do(eng, lambda e: e.drain(), inc=inc)

    def tok(self, eng):
        return self.cnt[eng]

    def replay(self, block):
        for ename in ENGS:
            deco = getattr(block, ename)
            q = self.q[ename]

            def body(e, _q=q):
                for item in _q:
                    k = item[0]
                    if k == "w":
                        e.wait_ge(item[1], item[2])
                    elif k == "o":
                        item[1](e)
                    else:
                        item[1](e).then_inc(item[2], item[3])
            deco(body)

    def nops(self):
        return sum(len(q) for q in self.q.values())


class Arena:

    def __init__(self, ap, total, name):
        self.ap = ap
        self.total = total
        self.off = 0
        self.hi = 0
        self.name = name
        self.marks = []

    def take(self, n, tag=""):
        assert self.off + n <= self.total, (
            f"{self.name} overflow: {self.off}+{n} > {self.total} ({tag})")
        r = self.ap[:, self.off:self.off + n]
        self.off += n
        self.hi = max(self.hi, self.off)
        return r

    def push(self):
        self.marks.append(self.off)

    def pop(self):
        self.off = self.marks.pop()


def mm3(P, ps, lh, ll, rh, rl, first, last, waits=(), inc=0):
    P.do("tensor", lambda e, o=ps, a=lh, b=rh, s=first: e.matmul(o, a, b, start=s, stop=False),
         waits=waits)
    P.do("tensor", lambda e, o=ps, a=lh, b=rl: e.matmul(o, a, b, start=False, stop=False))
    return P.do("tensor", lambda e, o=ps, a=ll, b=rh, s=last: e.matmul(o, a, b, start=False, stop=s),
                inc=inc)


class MK:
    def __init__(self, cfg):
        self.cfg = cfg


def build(cfg):
    T = cfg["T"]; D = cfg["D"]
    NCH = D // 128
    assert T % 512 == 0
    nc = bass.Bass("TRN2", target_bir_lowering=False)
    mk = MK(cfg)
    mk.nc = nc
    phases = list(cfg["phases"])

    dt = {}
    def din(name, shape, dtype=F32):
        dt[name] = nc.dram_tensor(name, shape, dtype, kind="ExternalInput").ap()
    def dout(name, shape, dtype=F32):
        dt[name] = nc.dram_tensor(name, shape, dtype, kind="ExternalOutput").ap()
    def dint(name, shape, dtype=F32):
        dt[name] = nc.dram_tensor(name, shape, dtype).ap()

    din("x0", [T, D])
    din("ident", [128, 128])
    din("identb", [128, 128], BF16)
    din("tril", [128, 128])
    dout("y", [T, D])
    if "ffn1" in phases or "ffn2" in phases:
        din("esel", [cfg["NEXP"], cfg["NEXP"], 128])
        din("wr", [D, cfg["NEXP"]])
        din("wsg", [D, cfg["DS"]]); din("wsu", [D, cfg["DS"]]); din("wsd", [cfg["DS"], D])
        din("weg", [cfg["NEXP"], D, cfg["DE"]]); din("weu", [cfg["NEXP"], D, cfg["DE"]])
        din("wed", [cfg["NEXP"], cfg["DE"], D])
    if "attn" in phases:
        S = cfg["S"]; NCORES = cfg["n_cores"]
        din("wq", [2, 128, 128]); din("wdkv", [2, 512, 128])
        din("wuk", [2, 512, 128]); din("wuv", [2, 512, 128])
        din("lngr", [2, 128, 512]); din("lnbr", [2, 128, 512])
        din("woh", [2, 128, 128])
        din("woT", [2, 128, D]); din("bo", [D])
        din("ridx", [1, 1], mybir.dt.int32)
        dint("ag1_in", [2, NCH, 128, T], BF16)
        dint("ag1_out", [NCORES, 2, NCH, 128, T], BF16)
        dint("xh_dram", [2, NCORES, 128, T], BF16)
        dint("o2buf", [2, 2, 128, S], BF16)
        NRSH = 2 if D // 512 > 2 else 1
        cfg["NRSH"] = NRSH
        dint("rs_in", [NRSH, NCORES * T, D // NRSH])
        dint("rs_out", [NRSH, T, D // NRSH])
        dint("cbuf", [2, 4, 128, S], BF16)
        dint("xspill", [NCH, 128, T])
        if cfg.get("dbg"):
            dt["dbg"] = nc.dram_tensor("dbg", [8, 16, 128, 134], F32, kind="ExternalOutput").ap()
    mk.dt = dt

    with ExitStack() as stack:
        en = stack.enter_context
        AF32_COLS = cfg.get("af32", 23000)
        ABF_COLS = cfg.get("abf", 59000)
        af32 = Arena(en(nc.sbuf_tensor("af32", [128, AF32_COLS], F32)).ap(), AF32_COLS, "af32")
        abf = Arena(en(nc.sbuf_tensor("abf", [128, ABF_COLS], BF16)).ap(), ABF_COLS, "abf")
        mk.af32, mk.abf = af32, abf
        mk.ps = [en(nc.psum_tensor(f"ps{i}", [128, 512], F32)).ap() for i in range(8)]
        P = Prog(nc, stack)
        mk.P = P

        mk.xres = af32.take(NCH * T, "xres")
        mk.ident = af32.take(128, "ident")
        mk.tril = af32.take(128, "tril")
        mk.esel = af32.take(8 * 128, "esel")
        mk.identb = abf.take(128, "identb")
        mk.hh = abf.take(NCH * T, "hh")
        mk.hl = abf.take(NCH * T, "hl")
        mk.frame0_f32 = af32.off
        mk.frame0_bf = abf.off
        mk.wpre = None
        mk.ffn_split_guard = []

        mk.sem_const = P.xsem("ld_const")
        P.do("sync", lambda e: e.dma_start(out=mk.ident, in_=dt["ident"][:, :]), inc=16, sem=mk.sem_const)
        P.do("sync", lambda e: e.dma_start(out=mk.tril, in_=dt["tril"][:, :]), inc=16, sem=mk.sem_const)
        if "esel" in dt:
            P.do("sync", lambda e: e.dma_start(out=mk.esel[0:8, :], in_=dt["esel"][:, :, :]), inc=16, sem=mk.sem_const)
        cdma = P.do("sync", lambda e: e.dma_start(out=mk.identb, in_=dt["identb"][:, :]), inc=16, sem=mk.sem_const)
        if "ridx" in dt:
            mk.ridx_sb = en(nc.sbuf_tensor("ridx_sb", [1, 1], mybir.dt.int32)).ap()
            rdma = P.do("sync", lambda e: e.dma_start(out=mk.ridx_sb[:, :], in_=dt["ridx"][:, :]),
                        inc=16, sem=mk.sem_const)
            def _ldrv(e, a=mk.ridx_sb):
                rv = e.value_load(a[0:1, 0:1])
                mk.rv2 = [e.snap(rv * 2 + 0), e.snap(rv * 2 + 1)]
                return e.nop()
            P.do("gpsimd", _ldrv, waits=[(mk.sem_const, rdma)])
        mk.const_d = (mk.sem_const, mk.sem_const.cnt)

        emit_load_x0(mk)

        for pi, ph in enumerate(phases):
            if ph in ("ffn1", "ffn2"):
                if mk.wpre is None:
                    g = {} if pi == 0 else {
                        "dma": [("vector", mk.x_ready["v"]),
                                ("scalar", mk.x_ready["s"]),
                                ("tensor", mk.x_ready["t"])],
                        "split": [("vector", mk.x_ready["v"]),
                                  ("scalar", mk.x_ready["s"]),
                                  ("tensor", mk.x_ready["t"])]}
                    emit_wpre(mk, guard=g)
                emit_ln(mk)
                emit_ffn(mk, tag=ph)
            elif ph == "attn":
                emit_ln(mk)
                emit_attn(mk, wpre_next=any(
                    p in ("ffn1", "ffn2") for p in phases[pi + 1:]))
            elif ph == "lnonly":
                emit_ln(mk)
                emit_h_to_x(mk)

        emit_store_y(mk)

        block = en(nc.Block())
        P.replay(block)
    mk.sbuf_use = (af32.hi * 4 + abf.hi * 2) / 1024.0
    return nc, mk


def emit_load_x0(mk):
    P, cfg = mk.P, mk.cfg
    T = cfg["T"]; D = cfg["D"]; NCH = D // 128
    nq = T // 128
    af32 = mk.af32
    af32.push()
    tmp = af32.take(2 * D, "x0tm")
    sems = [P.xsem("x0a"), P.xsem("x0b")]
    dma_tok = [0, 0]
    tr_done = {}
    ev_tok = {0: 0, 1: 0}
    for q in range(nq):
        slot = q % 2
        tmv = tmp[:, slot * D:(slot + 1) * D]
        w = [("tensor", tr_done[q - 2])] if q >= 2 else []
        dma_tok[slot] = P.do("sync", lambda e, o=tmv, i=mk.dt["x0"][q * 128:(q + 1) * 128, :]:
                             e.dma_start(out=o, in_=i), waits=w, inc=16, sem=sems[slot])
        for c in range(NCH):
            ps = mk.ps[c % 2]
            tk = P.do("tensor",
                      lambda e, o=ps[:, 0:128], i=tmv[:, c * 128:(c + 1) * 128], idn=mk.ident:
                      e.transpose(o, i, idn),
                      waits=[(sems[slot], dma_tok[slot]), ("vector", ev_tok[c % 2]),
                             (mk.sem_const, mk.const_d[1])], inc=1)
            ev_tok[c % 2] = P.do("vector",
                                 lambda e, o=mk.xres[:, c * T + q * 128: c * T + (q + 1) * 128],
                                 i=ps[:, 0:128]: e.tensor_copy(o, i),
                                 waits=[("tensor", tk)], inc=1)
        tr_done[q] = P.tok("tensor")
    P.drain("vector", inc=1)
    mk.x_ready = {"v": P.tok("vector"), "t": P.tok("tensor"), "s": P.tok("scalar")}
    af32.pop()


def emit_ln(mk):
    P, cfg = mk.P, mk.cfg
    T = cfg["T"]; D = cfg["D"]; NCH = D // 128
    nq = T // 128
    nbn = (D + 511) // 512
    af32 = mk.af32
    af32.push()
    xtm = af32.take(2 * D, "ln_xtm")
    sb = af32.take(32, "ln_sb")
    xr = mk.x_ready
    V = "vector"
    evA = {0: ("vector", 0), 1: ("vector", 0)}
    evB = {0: [("vector", 0)], 1: [("vector", 0)]}
    ap_guard = {0: 0, 1: 0}
    for q in range(nq):
        slot = q % 2
        xv = xtm[:, slot * D:(slot + 1) * D]
        hv = xv
        for c in range(NCH):
            ps = mk.ps[c % 2]
            tk = P.do("tensor", lambda e, o=ps[:, 0:128],
                      i=mk.xres[:, c * T + q * 128: c * T + (q + 1) * 128], idn=mk.ident:
                      e.transpose(o, i, idn),
                      waits=[("vector", xr["v"]), ("tensor", xr["t"]), evA[c % 2]], inc=1)
            ev = P.do(V, lambda e, o=xv[:, c * 128:(c + 1) * 128], i=ps[:, 0:128]:
                      e.tensor_copy(o, i), waits=[("tensor", tk)], inc=1)
            evA[c % 2] = ("vector", ev)
        P.drain(V, inc=1)
        for j in range(nbn):
            w = min(512, D - j * 512)
            P.do(V, lambda e, o=sb[:, j * 6:(j + 1) * 6], i=xv[:, j * 512:j * 512 + w]:
                 e.bn_stats(o, i))
        P.drain(V)
        P.do(V, lambda e, o=sb[:, 24:26], i=sb[:, 0:6 * nbn]: e.bn_aggr(o, i))
        P.drain(V)
        P.do(V, lambda e, o=sb[:, 26:27], i=sb[:, 25:26]:
             e.tensor_scalar(o, i, 1e-5, None, ALU.add))
        P.drain(V, inc=1)
        vpe_tok = P.tok(V)
        sq_tok = P.do("scalar", lambda e, o=sb[:, 27:28], i=sb[:, 26:27]:
                      e.activation(o, i, AF.Sqrt), waits=[("vector", vpe_tok)], inc=1)
        P.do(V, lambda e, o=sb[:, 28:29], i=sb[:, 27:28]: e.reciprocal(o, i),
             waits=[("scalar", sq_tok)])
        P.drain(V)
        P.do(V, lambda e, o=hv, i=xv, m=sb[:, 24:25], s=sb[:, 28:29]:
             e.tensor_scalar(o, i, m, s, ALU.subtract, ALU.mult),
             waits=[("scalar", ap_guard[slot])])
        P.drain(V, inc=1)
        ap_tok = P.tok(V)
        hi_tok = 0
        for c in range(NCH):
            ps = mk.ps[2 + c % 2]
            tk = P.do("tensor", lambda e, o=ps[:, 0:128], i=hv[:, c * 128:(c + 1) * 128],
                      idn=mk.ident: e.transpose(o, i, idn),
                      waits=[("vector", ap_tok)] + evB[c % 2], inc=1)
            hi_tok = P.do("scalar", lambda e, o=mk.hh[:, c * T + q * 128: c * T + (q + 1) * 128],
                          i=ps[:, 0:128]: e.activation(o, i, AF.Copy),
                          waits=[("tensor", tk)], inc=1)
            lo_tok = P.do(V, lambda e, o=mk.hl[:, c * T + q * 128: c * T + (q + 1) * 128],
                          a=ps[:, 0:128], b=mk.hh[:, c * T + q * 128: c * T + (q + 1) * 128]:
                          e.tensor_tensor(o, a, b, ALU.subtract),
                          waits=[("scalar", hi_tok)], inc=1)
            evB[c % 2] = [("scalar", hi_tok), ("vector", lo_tok)]
        ap_guard[slot] = hi_tok
    P.drain(V, inc=1)
    mk.h_ready = {"v": P.tok("vector"), "s": P.tok("scalar")}
    af32.pop()


def emit_h_to_x(mk):
    P, cfg = mk.P, mk.cfg
    T = cfg["T"]; NCH = cfg["D"] // 128
    for c in range(NCH):
        P.do("vector", lambda e, o=mk.xres[:, c * T:(c + 1) * T],
             a=mk.hh[:, c * T:(c + 1) * T], b=mk.hl[:, c * T:(c + 1) * T]:
             e.tensor_tensor(o, a, b, ALU.add),
             waits=[("vector", mk.h_ready["v"]), ("scalar", mk.h_ready["s"])], inc=1)
    P.drain("vector", inc=1)
    mk.x_ready = {"v": P.tok("vector"), "t": P.tok("tensor"), "s": P.tok("scalar")}


def _ffn_wraw_pad(cfg):
    T = cfg["T"]; TT_ = T // 512; NEXP = cfg["NEXP"]
    return 2 * TT_ * 512 + (T // 128) * NEXP + T + T + 16 + 8


def emit_wpre(mk, guard=None):
    P, cfg = mk.P, mk.cfg
    guard = guard or {}
    assert cfg["DS"] >= 256
    NCH = cfg["D"] // 128
    n = min(4, NCH)
    pad = _ffn_wraw_pad(cfg)
    wraw = mk.af32.ap[:, mk.frame0_f32 + pad: mk.frame0_f32 + pad + 1024]
    whil = mk.abf.ap[:, mk.frame0_bf: mk.frame0_bf + 2048]
    pre = dict(n=n, wsem=[P.xsem(f"w{i}") for i in range(8)], hi=[], lo=[])
    sg = list(guard.get("split", []))
    mk.ffn_split_guard = sg
    for k in range(n):
        rawv = wraw[:, k * 256: k * 256 + 256]
        dma_tok = P.do("sync", lambda e, o=rawv, i=mk.dt["wsg"][k * 128:(k + 1) * 128, 0:256]:
                       e.dma_start(out=o, in_=i), waits=list(guard.get("dma", [])),
                       inc=16, sem=pre["wsem"][k])
        hiv = whil[:, k * 512: k * 512 + 256]
        lov = whil[:, k * 512 + 256: k * 512 + 512]
        hi_tok = P.do("scalar", lambda e, o=hiv, i=rawv: e.activation(o, i, AF.Copy),
                      waits=[(pre["wsem"][k], dma_tok)] + sg, inc=1)
        lo_tok = P.do("vector", lambda e, o=lov, a=rawv, b=hiv:
                      e.tensor_tensor(o, a, b, ALU.subtract),
                      waits=[(pre["wsem"][k], dma_tok), ("scalar", hi_tok)] + sg, inc=1)
        pre["hi"].append(hi_tok)
        pre["lo"].append(lo_tok)
    mk.wpre = pre


class FFNState:
    def __init__(self, mk):
        self.mk = mk
        self.bank_free = [("vector", 0)] * 8
        self.wbuf_free = [0] * 8
        self.rawbuf_free_s = [0] * 8
        self.rawbuf_free_v = [0] * 8
        self.setsel = 0
        self.combrep_tok = 0
        self.pending = None
        self.sge_free = ("vector", 0)
        self.he_done = []
        self.pre = None


def _stream_mm(mk, st, w_dram_ap, kchunks, cols_rng, rhs_hi, rhs_lo, banks,
               n_m, t_tiles, extra_mm_waits=(), dma_guard=(), deferred=None,
               pre=None):
    P = mk.P
    c0, c1 = cols_rng
    gcols = c1 - c0
    wraw, whil = st.wraw, st.whil
    assert gcols <= 256
    nslots = 4
    if pre is not None:
        assert c0 == 0 and gcols == 256 and kchunks >= pre["n"]
    for k in range(kchunks):
        slot = k % nslots
        rawv = wraw[:, slot * 256: slot * 256 + gcols]
        hiv = whil[:, slot * 512: slot * 512 + gcols]
        lov = whil[:, slot * 512 + 256: slot * 512 + 256 + gcols]
        if pre is not None and k < pre["n"]:
            hi_tok, lo_tok = pre["hi"][k], pre["lo"][k]
            st.rawbuf_free_s[slot] = max(st.rawbuf_free_s[slot], hi_tok)
            st.rawbuf_free_v[slot] = max(st.rawbuf_free_v[slot], lo_tok)
        else:
            dma_tok = P.do("sync", lambda e, o=rawv, i=w_dram_ap[k * 128:(k + 1) * 128, c0:c1]:
                           e.dma_start(out=o, in_=i),
                           waits=[("scalar", st.rawbuf_free_s[slot]),
                                  ("vector", st.rawbuf_free_v[slot])] + list(dma_guard if k < 2 else ()),
                           inc=16, sem=st.wsem[slot])
            hi_tok = P.do("scalar", lambda e, o=hiv, i=rawv: e.activation(o, i, AF.Copy),
                          waits=[(st.wsem[slot], dma_tok), ("tensor", st.wbuf_free[slot])], inc=1)
            st.rawbuf_free_s[slot] = hi_tok
            lo_tok = P.do("vector", lambda e, o=lov, a=rawv, b=hiv: e.tensor_tensor(o, a, b, ALU.subtract),
                          waits=[(st.wsem[slot], dma_tok), ("scalar", hi_tok), ("tensor", st.wbuf_free[slot])],
                          inc=1)
            st.rawbuf_free_v[slot] = lo_tok
        if deferred is not None and k == min(1, kchunks - 1):
            deferred()
            deferred = None
        mmw = [("scalar", hi_tok), ("vector", lo_tok)] + (list(extra_mm_waits) if k == 0 else [])
        first = (k == 0)
        last = (k == kchunks - 1)
        for mi in range(n_m):
            lh = hiv[:, mi * 128:(mi + 1) * 128]
            ll = lov[:, mi * 128:(mi + 1) * 128]
            for ti in range(t_tiles):
                P.do("tensor", lambda e, o=banks[(mi, ti)], a=lh, b=rhs_hi(k, ti), s=first:
                     e.matmul(o, a, b, start=s, stop=False), waits=mmw)
                mmw = ()
                P.do("tensor", lambda e, o=banks[(mi, ti)], a=lh, b=rhs_lo(k, ti):
                     e.matmul(o, a, b, start=False, stop=False))
            for ti in range(t_tiles):
                is_last_of_chunk = (mi == n_m - 1 and ti == t_tiles - 1)
                P.do("tensor", lambda e, o=banks[(mi, ti)], a=ll, b=rhs_hi(k, ti), q=last:
                     e.matmul(o, a, b, start=False, stop=q),
                     inc=(1 if is_last_of_chunk else 0))
        st.wbuf_free[slot] = P.tok("tensor")
    return P.tok("tensor")


def _bank_waits(st, idxs):
    w = []
    for i in idxs:
        eng, tok = st.bank_free[i]
        if tok:
            w.append((eng, tok))
    return w


def emit_ffn(mk, tag):
    P, cfg = mk.P, mk.cfg
    T = cfg["T"]; D = cfg["D"]; DE = cfg["DE"]; DS = cfg["DS"]; NEXP = cfg["NEXP"]
    NCH = D // 128
    TT_ = T // 512
    af32, abf = mk.af32, mk.abf
    af32.push(); abf.push()

    assert abf.off == mk.frame0_bf
    whil = abf.take(2 * 1024, f"whilo")
    sge = af32.take(2 * TT_ * 512, f"sge")
    mx_ech = min(768, max(DE, DS)) // 128
    hehi = abf.take(mx_ech * T, f"hehi")
    helo = abf.take(mx_ech * T, f"helo")
    comb = af32.take((T // 128) * NEXP, f"comb")
    combT = af32.take(T, f"combT")
    combrep = af32.take(T, f"combrep")
    rtmp = af32.take(16, f"rtmp")
    rstat = af32.take(8, f"rstat")
    assert af32.off == mk.frame0_f32 + _ffn_wraw_pad(cfg)
    wraw = af32.take(2 * 512, f"wraw")

    pre = mk.wpre
    mk.wpre = None
    st = FFNState(mk)
    st.wraw, st.whil = wraw, whil
    st.wsem = pre["wsem"] if pre else [P.xsem(f"w{i}") for i in range(8)]
    st.pre = pre
    hrv, hrs = mk.h_ready["v"], mk.h_ready["s"]
    guard = [("vector", hrv), ("scalar", hrs)]
    rdguard = ([("scalar", pre["hi"][0]), ("vector", pre["lo"][0])]
               if pre else guard)

    emit_router(mk, st, comb, combT, rtmp, rstat, guard, rdguard)

    def cuts(total, maxc=768):
        cs = list(range(0, total, maxc)) + [total]
        return [(cs[i], cs[i + 1]) for i in range(len(cs) - 1) if cs[i] < cs[i + 1]]

    units = []
    for (a, b) in cuts(DS):
        units.append(dict(gw="wsg", uw="wsu", dw="wsd", ecols=(a, b), eidx=None))
    for eix in range(NEXP):
        for (a, b) in cuts(DE):
            units.append(dict(gw="weg", uw="weu", dw="wed", ecols=(a, b), eidx=eix))

    for u in units:
        emit_ffn_unit(mk, u, st, sge, combT, combrep, hehi, helo, guard)
    if st.pending is not None:
        cb = st.pending
        st.pending = None
        cb()

    P.drain("vector", inc=1)
    mk.x_ready = {"v": P.tok("vector"), "t": P.tok("tensor"), "s": P.tok("scalar")}
    af32.pop(); abf.pop()


def emit_ffn_unit(mk, u, st, sge, combT, combrep, hehi, helo, guard):
    P, cfg = mk.P, mk.cfg
    T = cfg["T"]; D = cfg["D"]
    NCH = D // 128
    TT_ = T // 512
    e0, e1 = u["ecols"]
    nech = (e1 - e0) // 128
    eix = u["eidx"]

    def wap(name):
        a = mk.dt[name]
        return a[eix] if eix is not None else a

    def flush():
        if st.pending is not None:
            cb = st.pending
            st.pending = None
            cb()

    if eix is not None and e0 == 0:
        flush()
        bs = st.setsel * 4
        for h in range(TT_):
            ps = mk.ps[bs + h]
            tk = P.do("tensor", lambda e, o=ps, a=mk.esel[0:8, eix * 128:(eix + 1) * 128],
                      b=combT[0:8, h * 512:(h + 1) * 512]:
                      e.matmul(o, a, b, start=True, stop=True),
                      waits=_bank_waits(st, [bs + h]) + [("vector", mk.combT_tok)], inc=1)
            ctok = P.do("vector", lambda e, o=combrep[:, h * 512:(h + 1) * 512], i=ps:
                        e.tensor_copy(o, i), waits=[("tensor", tk)], inc=1)
            st.bank_free[bs + h] = ("vector", ctok)
        P.drain("vector", inc=1)
        st.combrep_tok = P.tok("vector")
        st.setsel ^= 1
    if eix is None:
        st.combrep_tok = 0

    mgroups = [(g0, min(2, nech - g0)) for g0 in range(0, nech, 2)]
    for gi, (g0, nm) in enumerate(mgroups):
        bsA = st.setsel * 4
        banksA = {(mi, ti): mk.ps[bsA + mi * TT_ + ti] for mi in range(nm) for ti in range(TT_)}
        gw = _bank_waits(st, [bsA + i for i in range(nm * TT_)]) + guard
        pre_arg = st.pre
        st.pre = None
        t_gate = _stream_mm(mk, st, wap(u["gw"]), NCH, (e0 + g0 * 128, e0 + (g0 + nm) * 128),
                            lambda k, ti: mk.hh[:, k * T + ti * 512: k * T + ti * 512 + 512],
                            lambda k, ti: mk.hl[:, k * T + ti * 512: k * T + ti * 512 + 512],
                            banksA, nm, TT_, extra_mm_waits=gw, dma_guard=(),
                            deferred=st.pending, pre=pre_arg)
        st.pending = None
        holder = {}

        def silu_cb(banksA=banksA, t_gate=t_gate, bsA=bsA, nm=nm, holder=holder):
            tok = 0
            for mi in range(nm):
                for ti in range(TT_):
                    sl = mi * TT_ + ti
                    tok = P.do("scalar", lambda e, o=sge[:, sl * 512:(sl + 1) * 512],
                               i=banksA[(mi, ti)]: e.activation(o, i, AF.Silu),
                               waits=[("tensor", t_gate), st.sge_free], inc=1)
            for i in range(nm * TT_):
                st.bank_free[bsA + i] = ("scalar", tok)
            holder["silu"] = tok

        st.setsel ^= 1
        bsB = st.setsel * 4
        banksB = {(mi, ti): mk.ps[bsB + mi * TT_ + ti] for mi in range(nm) for ti in range(TT_)}
        uw = _bank_waits(st, [bsB + i for i in range(nm * TT_)])
        t_up = _stream_mm(mk, st, wap(u["uw"]), NCH, (e0 + g0 * 128, e0 + (g0 + nm) * 128),
                          lambda k, ti: mk.hh[:, k * T + ti * 512: k * T + ti * 512 + 512],
                          lambda k, ti: mk.hl[:, k * T + ti * 512: k * T + ti * 512 + 512],
                          banksB, nm, TT_, extra_mm_waits=uw, dma_guard=(),
                          deferred=silu_cb)

        def he_cb(banksB=banksB, t_up=t_up, bsB=bsB, nm=nm, g0=g0, holder=holder):
            V = "vector"
            for mi in range(nm):
                for ti in range(TT_):
                    sl = mi * TT_ + ti
                    P.do(V, lambda e, o=sge[:, sl * 512:(sl + 1) * 512],
                         a=sge[:, sl * 512:(sl + 1) * 512], b=banksB[(mi, ti)]:
                         e.tensor_tensor(o, a, b, ALU.mult),
                         waits=[("tensor", t_up), ("scalar", holder["silu"]),
                                ("vector", st.combrep_tok)], inc=1)
            P.drain(V, inc=1)
            het_tok = P.tok(V)
            for i in range(nm * TT_):
                st.bank_free[bsB + i] = ("vector", het_tok)
            if eix is not None:
                for mi in range(nm):
                    for ti in range(TT_):
                        sl = mi * TT_ + ti
                        P.do(V, lambda e, o=sge[:, sl * 512:(sl + 1) * 512],
                             a=sge[:, sl * 512:(sl + 1) * 512],
                             b=combrep[:, ti * 512:(ti + 1) * 512]:
                             e.tensor_tensor(o, a, b, ALU.mult))
                P.drain(V, inc=1)
                het_tok = P.tok(V)
            hih = 0
            for mi in range(nm):
                for ti in range(TT_):
                    sl = mi * TT_ + ti
                    hih = P.do("scalar", lambda e,
                               o=hehi[:, (g0 + mi) * T + ti * 512:(g0 + mi) * T + ti * 512 + 512],
                               i=sge[:, sl * 512:(sl + 1) * 512]: e.activation(o, i, AF.Copy),
                               waits=[("vector", het_tok)], inc=1)
            lot = 0
            for mi in range(nm):
                for ti in range(TT_):
                    sl = mi * TT_ + ti
                    lot = P.do(V, lambda e,
                               o=helo[:, (g0 + mi) * T + ti * 512:(g0 + mi) * T + ti * 512 + 512],
                               a=sge[:, sl * 512:(sl + 1) * 512],
                               b=hehi[:, (g0 + mi) * T + ti * 512:(g0 + mi) * T + ti * 512 + 512]:
                               e.tensor_tensor(o, a, b, ALU.subtract),
                               waits=[("scalar", hih)], inc=1)
            P.drain(V, inc=1)
            st.sge_free = ("vector", P.tok(V))
            st.he_done = [("vector", P.tok(V)), ("scalar", hih)]

        if gi == len(mgroups) - 1:
            he_cb()
        else:
            st.pending = he_cb
        st.setsel ^= 1

    he_waits = st.he_done

    for g0 in range(0, NCH, 2):
        bs = st.setsel * 4
        banks = {(mi, ti): mk.ps[bs + mi * TT_ + ti] for mi in range(2) for ti in range(TT_)}
        dwt = _bank_waits(st, [bs + i for i in range(2 * TT_)]) + he_waits
        t_down = _stream_mm(mk, st, wap(u["dw"])[e0:e1, :], nech, (g0 * 128, (g0 + 2) * 128),
                            lambda k, ti: hehi[:, k * T + ti * 512: k * T + ti * 512 + 512],
                            lambda k, ti: helo[:, k * T + ti * 512: k * T + ti * 512 + 512],
                            banks, 2, TT_, extra_mm_waits=dwt, dma_guard=(),
                            deferred=st.pending)
        st.pending = None

        def down_cb(banks=banks, t_down=t_down, bs=bs, g0=g0):
            for mi in range(2):
                for ti in range(TT_):
                    c = g0 + mi
                    acc = mk.xres[:, c * T + ti * 512: c * T + ti * 512 + 512]
                    ev = P.do("vector", lambda e, o=acc, i=banks[(mi, ti)], a=acc:
                              e.scalar_tensor_tensor(o, i, 0.5, a, ALU.mult, ALU.add),
                              waits=[("tensor", t_down)], inc=1)
                    st.bank_free[bs + mi * TT_ + ti] = ("vector", ev)
            P.drain("vector", inc=1)

        st.pending = down_cb
        st.setsel ^= 1


def emit_router(mk, st, comb, combT, rtmp, rstat, guard, rdguard):
    P, cfg = mk.P, mk.cfg
    T = cfg["T"]; D = cfg["D"]; NEXP = cfg["NEXP"]
    NCH = D // 128
    nq = T // 128
    nhalf = T // 512
    V = "vector"
    abf = mk.abf
    abf.push()
    wr_hi = abf.take(NCH * NEXP, "wrhi")
    wr_lo = abf.take(NCH * NEXP, "wrlo")
    rraw = st.wraw[:, 0:NCH * NEXP]
    d_tok = 0
    for c in range(NCH):
        d_tok = P.do("sync", lambda e, o=rraw[:, c * NEXP:(c + 1) * NEXP],
                     i=mk.dt["wr"][c * 128:(c + 1) * 128, :]:
                     e.dma_start(out=o, in_=i), waits=rdguard, inc=16, sem=st.wsem[0])
    h_tok = P.do("scalar", lambda e, o=wr_hi, i=rraw: e.activation(o, i, AF.Copy),
                 waits=[(st.wsem[0], d_tok)] + mk.ffn_split_guard, inc=1)
    l_tok = P.do("vector", lambda e, o=wr_lo, a=rraw, b=wr_hi: e.tensor_tensor(o, a, b, ALU.subtract),
                 waits=[("scalar", h_tok)] + mk.ffn_split_guard, inc=1)
    st.rawbuf_free_s[0] = h_tok
    st.rawbuf_free_v[0] = l_tok
    for h in range(nhalf):
        for k in range(NCH):
            mm3(P, mk.ps[h][0:NEXP, :], wr_hi[:, k * NEXP:(k + 1) * NEXP],
                wr_lo[:, k * NEXP:(k + 1) * NEXP],
                mk.hh[:, k * T + h * 512: k * T + h * 512 + 512],
                mk.hl[:, k * T + h * 512: k * T + h * 512 + 512],
                first=(k == 0), last=(k == NCH - 1),
                waits=([("vector", l_tok), ("scalar", h_tok)] + guard) if k == 0 else ())
    r_tok = P.drain("tensor", inc=1)
    ev = 0
    for h in range(nhalf):
        ev = P.do(V, lambda e, o=combT[0:NEXP, h * 512:(h + 1) * 512], i=mk.ps[h][0:NEXP, :]:
                  e.tensor_copy(o, i), waits=[("tensor", r_tok)], inc=1)
    P.drain(V, inc=1)
    rT_tok = P.tok(V)
    tr_ev = {0: 0, 1: 0}
    for q in range(nq):
        ps = mk.ps[2 + (q % 2)]
        tk = P.do("tensor", lambda e, o=ps[:, 0:NEXP], i=combT[0:NEXP, q * 128:(q + 1) * 128],
                  idn=mk.ident[0:NEXP, 0:NEXP]: e.transpose(o, i, idn),
                  waits=[("vector", rT_tok), ("vector", tr_ev[q % 2])], inc=1)
        tr_ev[q % 2] = P.do(V, lambda e, o=comb[:, q * NEXP:(q + 1) * NEXP], i=ps[:, 0:NEXP]:
                            e.tensor_copy(o, i), waits=[("tensor", tk)], inc=1)
    P.drain(V, inc=1)
    M1, M2, RC = 0, 1, 2
    for q in range(nq):
        cm = comb[:, q * NEXP:(q + 1) * NEXP]
        tm = rtmp[:, 0:NEXP]
        P.do(V, lambda e, o=rstat[:, M1:M1 + 1], i=cm: e.reduce_max(o, i, axis=AX.X, negate=True))
        P.drain(V, inc=1)
        nmax_tok = P.tok(V)
        ex_tok = P.do("scalar", lambda e, o=cm, i=cm, b=rstat[:, M1:M1 + 1], s=rstat[:, RC:RC + 1]:
                      e.activation(o, i, AF.Exp, bias=b, accum_out=s),
                      waits=[("vector", nmax_tok)], inc=1)
        P.do(V, lambda e, o=rstat[:, RC:RC + 1], i=rstat[:, RC:RC + 1]: e.reciprocal(o, i),
             waits=[("scalar", ex_tok)])
        P.drain(V)
        P.do(V, lambda e, o=cm, i=cm, s=rstat[:, RC:RC + 1]: e.tensor_scalar(o, i, s, None, ALU.mult))
        P.drain(V)
        P.do(V, lambda e, o=rstat[:, M1:M1 + 1], i=cm: e.reduce_max(o, i, axis=AX.X))
        P.drain(V)
        P.do(V, lambda e, o=tm, i=cm, s=rstat[:, M1:M1 + 1]: e.tensor_scalar(o, i, s, None, ALU.is_ge))
        P.drain(V)
        P.do(V, lambda e, o=tm, i=tm, b=cm: e.scalar_tensor_tensor(o, i, -1e30, b, ALU.mult, ALU.add))
        P.drain(V)
        P.do(V, lambda e, o=rstat[:, M2:M2 + 1], i=tm: e.reduce_max(o, i, axis=AX.X))
        P.drain(V)
        P.do(V, lambda e, o=tm, i=cm, s=rstat[:, M2:M2 + 1]: e.tensor_scalar(o, i, s, None, ALU.is_ge))
        P.drain(V)
        P.do(V, lambda e, o=cm, a=cm, b=tm: e.tensor_tensor(o, a, b, ALU.mult))
        P.drain(V, inc=1)
    comb_tok = P.tok(V)
    tr_ev = {0: 0, 1: 0}
    for q in range(nq):
        ps = mk.ps[2 + (q % 2)]
        tk = P.do("tensor", lambda e, o=ps[0:NEXP, 0:128], i=comb[:, q * NEXP:(q + 1) * NEXP],
                  idn=mk.ident: e.transpose(o, i, idn),
                  waits=[("vector", comb_tok), ("vector", tr_ev[q % 2])], inc=1)
        tr_ev[q % 2] = P.do(V, lambda e, o=combT[0:NEXP, q * 128:(q + 1) * 128],
                            i=ps[0:NEXP, 0:128]: e.tensor_copy(o, i),
                            waits=[("tensor", tk)], inc=1)
    P.drain(V, inc=1)
    mk.combT_tok = P.tok(V)
    for b in set(list(range(nhalf)) + [2, 3]):
        st.bank_free[b] = ("vector", mk.combT_tok)
    abf.pop()


def emit_attn(mk, wpre_next=False):
    P, cfg = mk.P, mk.cfg
    T = cfg["T"]; D = cfg["D"]; S = cfg["S"]; NCO = cfg["n_cores"]
    NCH = D // 128
    LCH = 4
    GQT = min(16, S // 128)
    V = "vector"
    hrv, hrs = mk.h_ready["v"], mk.h_ready["s"]
    af32, abf = mk.af32, mk.abf
    af32.push(); abf.push()

    QTW = LCH * GQT * 128
    if NCH * T >= 2 * S:
        xh_hi, xh_lo = mk.hh[:, 0:S], mk.hh[:, S:2 * S]
    else:
        xh_hi, xh_lo = abf.take(S, "xh_hi"), abf.take(S, "xh_lo")
    if NCH * T >= 2 * QTW:
        qt_hi, qt_lo = mk.hl[:, 0:QTW], mk.hl[:, QTW:2 * QTW]
    else:
        qt_hi, qt_lo = abf.take(QTW, "qt_hi"), abf.take(QTW, "qt_lo")
    FW = S + 4 * 1024 + GQT * 128
    if NCH * T >= FW:
        fs = mk.xres
        mk.fs_aliased = True
        mk.fs_width = NCH * T
    else:
        fs = af32.take(FW, "attn_f32")
        mk.fs_aliased = False
        mk.fs_width = FW
    vbuf = fs[:, 0:S]
    span = fs[:, S:S + 1024]
    epan = fs[:, S + 1024:S + 2048]
    ohacc = fs[:, S + 2048:S + 2048 + GQT * 128]
    sp2o = S + 2048 + GQT * 128
    span2 = fs[:, sp2o:sp2o + 1024]
    epan2 = fs[:, sp2o + 1024:sp2o + 2048]
    R = dict(
        xh_hi=xh_hi, xh_lo=xh_lo, qt_hi=qt_hi, qt_lo=qt_lo,
        vbuf=vbuf, span=span, epan=epan, span2=span2, epan2=epan2, ohacc=ohacc,
        fsS=fs[:, S:S + 2048], fs=fs,
        cwork=af32.take(4 * 512, "cwork"),
        wwork=af32.take(3 * 512, "wwork"),
        astat=af32.take(64, "astat"),
        cstage=abf.take(2 * LCH * 512, "cstage"),
        cpan=[dict(hi=abf.take(LCH * 1024, f"cp{s}h"), lo=abf.take(LCH * 1024, f"cp{s}l"))
              for s in range(2)],
        whi=abf.take(1664, "awhi"),
        wlo=abf.take(1664, "awlo"),
        o2st=abf.take(2 * min(8, GQT) * 128, "o2st"),
    )
    R["cstage_hi"] = R["cstage"][:, 0:LCH * 512]
    R["cstage_lo"] = R["cstage"][:, LCH * 512:2 * LCH * 512]
    rg = [list(range(NCO))]
    dtb = mk.dt

    mk.attn_ps_guard = {}
    mk.attn_xh_guard = []
    mk.attn_qt_guard = []
    mk.attn_head_guard = {0: [("vector", mk.x_ready["v"]), ("scalar", mk.x_ready["s"])], 1: []}

    ag1w = P.xsem("ag1w")
    for hl, src in ((0, mk.hh), (1, mk.hl)):
        for c in range(NCH):
            P.do("gpsimd", lambda e, o=dtb["ag1_in"][hl, c],
                 i=src[:, c * T:(c + 1) * T]: e.dma_start(out=o, in_=i),
                 waits=[("vector", hrv), ("scalar", hrs)], inc=16, sem=ag1w)
    spx = P.xsem("spx")
    for c in range(NCH):
        P.do("sync", lambda e, o=dtb["xspill"][c], i=mk.xres[:, c * T:(c + 1) * T]:
             e.dma_start(out=o, in_=i), waits=[("vector", mk.x_ready["v"])], inc=16, sem=spx)
    mk.spx = spx
    mk.attn_xh_guard = [(ag1w, ag1w.cnt)]
    mk.attn_head_guard[0] += [(spx, spx.cnt), (ag1w, ag1w.cnt)]

    cc = P.xsem("cc1")
    P.do("gpsimd", lambda e: e.collective_compute(
        "AllGather", ALU.bypass, replica_groups=rg,
        ins=[dtb["ag1_in"]], outs=[dtb["ag1_out"]]),
        waits=[(ag1w, ag1w.cnt)], inc=1, sem=cc)

    mk.dbg_sem = P.xsem("dbgd") if cfg.get("dbg") else None
    mk.attn_sb_dma_reads = []
    o2w = P.xsem("o2w")
    for hp in range(2):
        _attn_head(mk, hp, cc, o2w, R)

    cc2 = P.xsem("cc2")
    mk.cc2 = cc2

    def coll_cb(pi):
        P.do("gpsimd", lambda e, h=pi: e.collective_compute(
            "ReduceScatter", ALU.add, replica_groups=rg,
            ins=[dtb["rs_in"][h]], outs=[dtb["rs_out"][h]]),
            waits=[mk.rs_ready[pi]], inc=1, sem=cc2)

    _emit_wo_partial(mk, R, o2w, coll_cb)
    mk.attn_done = {"v": P.tok("vector"), "s": P.tok("scalar"), "t": P.tok("tensor")}
    af32.pop(); abf.pop()
    if wpre_next:
        ag = ([("tensor", mk.attn_done["t"]), ("vector", mk.attn_done["v"]),
               ("scalar", mk.attn_done["s"])] + mk.rs_done
              + mk.attn_sb_dma_reads)
        emit_wpre(mk, guard={"dma": ag, "split": ag})
    emit_wo_final(mk)


def _attn_head(mk, hp, cc, ag2w, R):
    P, cfg = mk.P, mk.cfg
    T = cfg["T"]; S = cfg["S"]; NCO = cfg["n_cores"]
    LCH = 4
    GQT = min(16, S // 128)
    NGRP = S // (GQT * 128)
    NKT = S // 128
    V = "vector"
    dtb = mk.dt
    SC = float(1.0 / np.sqrt(512.0))
    whi, wlo, wwork, cwork, astat = R["whi"], R["wlo"], R["wwork"], R["cwork"], R["astat"]
    csh, csl = R["cstage_hi"], R["cstage_lo"]

    wld = P.xsem(f"wld{hp}")
    guard0 = mk.attn_head_guard.get(hp, [])

    def split_from_psum(ps, dst_hi, dst_lo, tk, w=128):
        hi = P.do("scalar", lambda e, o=dst_hi, i=ps[:, 0:w]: e.activation(o, i, AF.Copy),
                  waits=[("tensor", tk)], inc=1)
        lo = P.do(V, lambda e, o=dst_lo, a=ps[:, 0:w], b=dst_hi:
                  e.tensor_tensor(o, a, b, ALU.subtract), waits=[("scalar", hi)], inc=1)
        return hi, lo

    tok = 0
    for lc in range(4):
        tok = P.do("sync", lambda e, o=wwork[:, lc * 128:(lc + 1) * 128],
                   i=dtb["wdkv"][hp][lc * 128:(lc + 1) * 128, :]: e.dma_start(out=o, in_=i),
                   waits=[(wld, wld.cnt)] + guard0, inc=16, sem=wld)
    hi_t = lo_t = 0
    for lc in range(4):
        ps = mk.ps[4 + lc % 2]
        tk = P.do("tensor", lambda e, o=ps[:, 0:128], i=wwork[:, lc * 128:(lc + 1) * 128],
                  idn=mk.ident: e.transpose(o, i, idn),
                  waits=[(wld, tok), (mk.sem_const, mk.const_d[1])] + mk.attn_ps_guard.get(4 + lc % 2, []),
                  inc=1)
        hi_t, lo_t = split_from_psum(ps, whi[:, lc * 128:(lc + 1) * 128],
                                     wlo[:, lc * 128:(lc + 1) * 128], tk)
        mk.attn_ps_guard[4 + lc % 2] = [("scalar", hi_t), ("vector", lo_t)]
    tok2 = P.do("sync", lambda e, o=wwork[:, 512:640], i=dtb["wq"][hp]:
                e.dma_start(out=o, in_=i),
                waits=[(wld, wld.cnt), ("scalar", hi_t), ("vector", lo_t)],
                inc=16, sem=wld)
    for lc in range(4):
        tok2 = P.do("sync", lambda e, o=wwork[:, 640 + lc * 128: 768 + lc * 128],
                    i=dtb["wuk"][hp][lc * 128:(lc + 1) * 128, :]: e.dma_start(out=o, in_=i),
                    waits=[(wld, wld.cnt)], inc=16, sem=wld)
    ps = mk.ps[6]
    tk = P.do("tensor", lambda e, o=ps[:, 0:128], i=wwork[:, 512:640], idn=mk.ident:
              e.transpose(o, i, idn), waits=[(wld, tok2)] + mk.attn_ps_guard.get(6, []), inc=1)
    qh, ql = split_from_psum(ps, csh[:, 0:128], csl[:, 0:128], tk)
    qh2, ql2 = qh, ql
    for lc in range(4):
        ps = mk.ps[7] if lc % 2 else mk.ps[6]
        tk = P.do("tensor", lambda e, o=ps[:, 0:128], i=wwork[:, 640 + lc * 128:768 + lc * 128],
                  idn=mk.ident: e.transpose(o, i, idn),
                  waits=[(wld, tok2), ("scalar", qh), ("vector", ql)]
                  + mk.attn_ps_guard.get(7 if lc % 2 else 6, []), inc=1)
        qh2, ql2 = split_from_psum(ps, csh[:, 128 + lc * 128:256 + lc * 128],
                                   csl[:, 128 + lc * 128:256 + lc * 128], tk)
        mk.attn_ps_guard[7 if lc % 2 else 6] = [("scalar", qh2), ("vector", ql2)]
    ps_a = mk.ps[5]
    mm3(P, ps_a, csh[:, 0:128], csl[:, 0:128], csh[:, 128:640], csl[:, 128:640],
        first=True, last=True,
        waits=[("scalar", qh2), ("vector", ql2)] + mk.attn_ps_guard.get(5, []), inc=1)
    a_tok = P.tok("tensor")
    af_ = wwork[:, 1024:1536]
    ev = P.do("scalar", lambda e, o=af_, i=ps_a: e.activation(o, i, AF.Copy, scale=SC),
              waits=[("tensor", a_tok)], inc=1)
    P.drain("scalar")
    ah = P.do("scalar", lambda e, o=whi[:, 512:1024], i=af_: e.activation(o, i, AF.Copy),
              inc=1)
    al = P.do(V, lambda e, o=wlo[:, 512:1024], a=af_, b=whi[:, 512:1024]:
              e.tensor_tensor(o, a, b, ALU.subtract), waits=[("scalar", ah)], inc=1)
    mk.attn_ps_guard[5] = [("scalar", ah)]
    tok3 = 0
    for lc in range(4):
        tok3 = P.do("sync", lambda e, o=wwork[:, lc * 128:(lc + 1) * 128],
                    i=dtb["wuv"][hp][lc * 128:(lc + 1) * 128, :]: e.dma_start(out=o, in_=i),
                    waits=[(wld, wld.cnt), ("scalar", hi_t), ("vector", lo_t)],
                    inc=16, sem=wld)
    uh = P.do("scalar", lambda e, o=whi[:, 1024:1536], i=wwork[:, 0:512]:
              e.activation(o, i, AF.Copy), waits=[(wld, tok3)], inc=1)
    ul = P.do(V, lambda e, o=wlo[:, 1024:1536], a=wwork[:, 0:512], b=whi[:, 1024:1536]:
              e.tensor_tensor(o, a, b, ALU.subtract), waits=[("scalar", uh)], inc=1)
    tok4 = P.do("sync", lambda e, o=wwork[:, 512:640], i=dtb["woh"][hp]:
                e.dma_start(out=o, in_=i),
                waits=[(wld, wld.cnt), ("scalar", qh2), ("vector", ql2)],
                inc=16, sem=wld)
    ps = mk.ps[4]
    tk = P.do("tensor", lambda e, o=ps[:, 0:128], i=wwork[:, 512:640], idn=mk.ident:
              e.transpose(o, i, idn), waits=[(wld, tok4)] + mk.attn_ps_guard.get(4, []), inc=1)
    oh_h, oh_l = split_from_psum(ps, whi[:, 1536:1664], wlo[:, 1536:1664], tk)
    mk.attn_ps_guard[4] = [("scalar", oh_h), ("vector", oh_l)]
    lnr = wwork[:, 512:1024]
    lbr = cwork[:, 1536:2048]
    P.do("sync", lambda e, o=lnr, i=dtb["lngr"][hp]: e.dma_start(out=o, in_=i),
         waits=[(wld, wld.cnt), ("scalar", oh_h), ("vector", oh_l)], inc=16, sem=wld)
    t6 = P.do("sync", lambda e, o=lbr, i=dtb["lnbr"][hp]: e.dma_start(out=o, in_=i),
              waits=[(wld, wld.cnt)], inc=16, sem=wld)

    d2d = P.xsem(f"d2d{hp}")
    for hl in range(2):
        P.do("gpsimd", lambda e, ll=hl, hh=hp:
             e.dma_start(out=dtb["xh_dram"][ll],
                         in_=dtb["ag1_out"][:, ll][:, bass.ds(mk.rv2[hh], 1)]
                         .rearrange("j o p t -> (j o) p t")),
             waits=[(cc, cc.cnt), (d2d, d2d.cnt)] + mk.attn_xh_guard, inc=16, sem=d2d)
    xhld = P.xsem(f"xhld{hp}")
    xh_hi, xh_lo = R["xh_hi"], R["xh_lo"]
    for j in range(NCO):
        for hl, dst in ((0, xh_hi), (1, xh_lo)):
            P.do("sync", lambda e, o=dst[:, j * T:(j + 1) * T], jj=j, ll=hl:
                 e.dma_start(out=o, in_=dtb["xh_dram"][ll, jj]),
                 waits=[(d2d, d2d.cnt)] + mk.attn_xh_guard, inc=16, sem=xhld)
    xh_tok = xhld.cnt
    mk.last_xhld = (xhld, xh_tok)

    vbuf = R["vbuf"]
    cbw = P.xsem(f"cbw{hp}")
    cguard = {0: [], 1: []}
    tr_guard = {2: [], 3: []}
    vg_guard = {6: [], 7: []}
    cn_rd = {0: 0, 1: 0}
    cn_rd_t = {0: 0, 1: 0}
    nst = min(4, NKT)
    stage_dma = [0]
    cprev = None

    def c_flush(base_kt):
        for lc in range(LCH):
            for hl, stg in ((0, csh), (1, csl)):
                stage_dma[0] = P.do("gpsimd", lambda e,
                                    o=dtb["cbuf"][hl, lc][:, base_kt * 128:(base_kt + nst) * 128],
                                    i=stg[:, lc * nst * 128:(lc + 1) * nst * 128]:
                                    e.dma_start(out=o, in_=i),
                                    waits=tr_guard[2] + tr_guard[3]
                                    + getattr(mk, "attn_prev_pan", []),
                                    inc=16, sem=cbw)

    def c_stageB(kt):
        par = kt % 2
        cn3 = cwork[:, par * 512:(par + 1) * 512]
        bank = 2 + par
        ps_t = mk.ps[bank]
        stcol = (kt % nst) * 128
        tk = 0
        for lc in range(LCH):
            tk = P.do("tensor", lambda e, o=ps_t[:, lc * 128:(lc + 1) * 128],
                      a=cn3[:, lc * 128:(lc + 1) * 128], b=mk.ident,
                      s=(lc == 0), q=(lc == LCH - 1):
                      e.matmul(o, a, b, is_transpose=True, start=s, stop=q),
                      waits=(([("vector", cn_rd[par])] + tr_guard.get(bank, []))
                             if lc == 0 else []), inc=(1 if lc == LCH - 1 else 0))
        cn_rd_t[par] = tk
        his = 0
        for lc in range(LCH):
            his = P.do("scalar", lambda e,
                       o=csh[:, lc * nst * 128 + stcol: lc * nst * 128 + stcol + 128],
                       i=ps_t[:, lc * 128:(lc + 1) * 128]: e.activation(o, i, AF.Copy),
                       waits=([("tensor", tk), (cbw, stage_dma[0])] if lc == 0 else []),
                       inc=1)
        los = 0
        for lc in range(LCH):
            los = P.do(V, lambda e,
                       o=csl[:, lc * nst * 128 + stcol: lc * nst * 128 + stcol + 128],
                       a=ps_t[:, lc * 128:(lc + 1) * 128],
                       b=csh[:, lc * nst * 128 + stcol: lc * nst * 128 + stcol + 128]:
                       e.tensor_tensor(o, a, b, ALU.subtract),
                       waits=[("scalar", his), (cbw, stage_dma[0])], inc=1)
        tr_guard[bank] = [("vector", los), ("scalar", his)]
        ps_v = mk.ps[6 + par]
        for lc in range(LCH):
            mm3(P, ps_v[:, 0:128], csh[:, lc * nst * 128 + stcol: lc * nst * 128 + stcol + 128],
                csl[:, lc * nst * 128 + stcol: lc * nst * 128 + stcol + 128],
                whi[:, 1024 + lc * 128:1152 + lc * 128], wlo[:, 1024 + lc * 128:1152 + lc * 128],
                first=(lc == 0), last=(lc == LCH - 1),
                waits=(([("scalar", his), ("vector", los)] + vg_guard.get(6 + par, []))
                       if lc == 0 else []), inc=(1 if lc == LCH - 1 else 0))
        vtk2 = P.tok("tensor")
        ev = P.do(V, lambda e, o=vbuf[:, kt * 128:(kt + 1) * 128], i=ps_v[:, 0:128]:
                  e.tensor_copy(o, i), waits=[("tensor", vtk2)], inc=1)
        vg_guard[6 + par] = [("vector", ev)]

    for kt in range(NKT):
        par = kt % 2
        ps_c = mk.ps[par]
        w0 = ([(xhld, xh_tok), (wld, t6), ("scalar", oh_h), ("vector", oh_l),
               ("vector", ul), ("vector", al)] if kt == 0 else [])
        mm3(P, ps_c, xh_hi[:, kt * 128:(kt + 1) * 128], xh_lo[:, kt * 128:(kt + 1) * 128],
            whi[:, 0:512], wlo[:, 0:512], first=True, last=True,
            waits=w0 + cguard.get(par, []), inc=1)
        ctk = P.tok("tensor")
        sb0 = 14 * par
        P.do(V, lambda e, o=astat[:, sb0:sb0 + 6], i=ps_c: e.bn_stats(o, i),
             waits=[("tensor", ctk)])
        P.drain(V)
        P.do(V, lambda e, o=astat[:, sb0 + 6:sb0 + 8], i=astat[:, sb0:sb0 + 6]:
             e.bn_aggr(o, i))
        P.drain(V)
        P.do(V, lambda e, o=astat[:, sb0 + 8:sb0 + 9], i=astat[:, sb0 + 7:sb0 + 8]:
             e.tensor_scalar(o, i, 1e-5, None, ALU.add))
        P.drain(V, inc=1)
        vtk = P.tok(V)
        stk = P.do("scalar", lambda e, o=astat[:, sb0 + 9:sb0 + 10],
                   i=astat[:, sb0 + 8:sb0 + 9]:
                   e.activation(o, i, AF.Sqrt), waits=[("vector", vtk)], inc=1)
        P.do(V, lambda e, o=astat[:, sb0 + 10:sb0 + 11], i=astat[:, sb0 + 9:sb0 + 10]:
             e.reciprocal(o, i), waits=[("scalar", stk)])
        P.drain(V)
        P.do(V, lambda e, o=astat[:, sb0 + 11:sb0 + 12], a=astat[:, sb0 + 6:sb0 + 7],
             b=astat[:, sb0 + 10:sb0 + 11]:
             e.scalar_tensor_tensor(o, a, -1.0, b, ALU.mult, ALU.mult))
        P.drain(V, inc=1)
        st_tok = P.tok(V)
        if cprev is not None:
            c_stageB(cprev)
            if (cprev + 1) % nst == 0:
                c_flush(cprev + 1 - nst)
        cn3 = cwork[:, par * 512:(par + 1) * 512]
        ap_tok = P.do("scalar", lambda e, o=cn3, i=ps_c, sc=astat[:, sb0 + 10:sb0 + 11],
                      bi=astat[:, sb0 + 11:sb0 + 12]:
                      e.activation(o, i, AF.Identity, scale=sc, bias=bi),
                      waits=[("vector", st_tok), ("tensor", cn_rd_t.get(par, 0))], inc=1)
        cguard[par] = [("scalar", ap_tok)]
        P.do(V, lambda e, o=cn3, a=cn3, b=lnr: e.tensor_tensor(o, a, b, ALU.mult),
             waits=[("scalar", ap_tok)])
        P.drain(V)
        P.do(V, lambda e, o=cn3, a=cn3, b=lbr: e.tensor_tensor(o, a, b, ALU.add))
        P.drain(V, inc=1)
        cn_rd[par] = P.tok(V)
        cprev = kt
    if cprev is not None:
        c_stageB(cprev)
        c_flush(cprev + 1 - nst)
    cb_done = cbw.cnt
    mk.attn_sb_dma_reads.append((cbw, cbw.cnt))
    vb_done = P.drain(V, inc=1)

    qt_hi, qt_lo = R["qt_hi"], R["qt_lo"]
    span, epan, ohacc = R["span"], R["epan"], R["ohacc"]
    o2st = R["o2st"]
    cpan = R["cpan"]
    cpsem = [P.xsem(f"cp{hp}a"), P.xsem(f"cp{hp}b")]
    cpan_guard = [[], []]
    sc_guard = {0: [], 1: []}
    et_guard = {2: [], 3: []}
    oh_ps_guard = {4: [], 5: []}
    qtg = mk.attn_qt_guard
    prev_exp = {0: 0, 1: 0}
    prev_epan_rd = {0: 0, 1: 0}
    eT_free = {0: [], 1: []}
    prev_ag2 = ag2w.cnt
    span2, epan2 = R["span2"], R["epan2"]
    fsS = R["fsS"]
    mk.cstage_sfree = getattr(mk, "cstage_sfree", {})
    GW = GQT * 128
    for g in range(NGRP):
        q0 = g * GW
        qsp_tok = 0
        for lc in range(LCH):
            for qs in range(GW // 512):
                ps_q = mk.ps[qs % 2]
                mm3(P, ps_q, whi[:, 512 + lc * 128:640 + lc * 128],
                    wlo[:, 512 + lc * 128:640 + lc * 128],
                    xh_hi[:, q0 + qs * 512: q0 + (qs + 1) * 512],
                    xh_lo[:, q0 + qs * 512: q0 + (qs + 1) * 512],
                    first=True, last=True,
                    waits=sc_guard.get(qs % 2, []) + qtg + [("vector", vb_done)], inc=1)
                qtk = P.tok("tensor")
                col = lc * GW + qs * 512
                hq = P.do("scalar", lambda e, o=qt_hi[:, col:col + 512], i=ps_q:
                          e.activation(o, i, AF.Copy), waits=[("tensor", qtk)], inc=1)
                lq = P.do(V, lambda e, o=qt_lo[:, col:col + 512], a=ps_q, b=qt_hi[:, col:col + 512]:
                          e.tensor_tensor(o, a, b, ALU.subtract), waits=[("scalar", hq)], inc=1)
                sc_guard[qs % 2] = [("scalar", hq), ("vector", lq)]
        qt_done = [("scalar", P.tok("scalar")), ("vector", P.tok("vector"))]
        P.do(V, lambda e, o=astat[:, 16:16 + GQT]: e.memset(o, -1e30), waits=qt_done)
        P.do(V, lambda e, o=astat[:, 48:48 + GQT]: e.memset(o, 0.0))
        iz = P.do(V, lambda e, o=ohacc: e.memset(o, 0.0),
                  waits=[("tensor", P.tok("tensor"))], inc=1)
        P.drain(V, inc=1)
        mslot = {j: 0 for j in range(GQT)}

        def av_stage(pv, iz=iz):
            ii, p, j, kext = pv["ii"], pv["p"], pv["j"], pv["kext"]
            par = pv["par"]
            sc0 = 8 * par
            epv = epan if par == 0 else epan2
            ps_o = mk.ps[4 + ii % 2]
            nkb = (kext + 127) // 128
            groups = [(gg * 4, min(4, nkb - gg * 4)) for gg in range((nkb + 3) // 4)]
            evts = {}

            def emit_T(gi):
                kb0, n4 = groups[gi]
                bank = 2 + gi % 2
                ps_t = mk.ps[bank]
                tk = 0
                for u in range(n4):
                    kb = kb0 + u
                    w0 = ([("scalar", pv["ex"])] + et_guard.get(bank, [])) if u == 0 else []
                    tk = P.do("tensor", lambda e, o=ps_t[:, u * 128:(u + 1) * 128],
                              a=epv[:, kb * 128:(kb + 1) * 128], b=mk.ident,
                              s=(u == 0), q=(u == n4 - 1):
                              e.matmul(o, a, b, is_transpose=True, start=s, stop=q),
                              waits=w0, inc=(1 if u == n4 - 1 else 0))
                eng = "vector" if gi % 2 == 0 else "scalar"
                slot = cwork[:, (gi % 2) * 512:(gi % 2) * 512 + 512]
                if eng == "vector":
                    evt = P.do(V, lambda e, o=slot[:, 0:n4 * 128], i=ps_t[:, 0:n4 * 128]:
                               e.tensor_copy(o, i),
                               waits=[("tensor", tk)] + eT_free[gi % 2], inc=1)
                else:
                    evt = P.do("scalar", lambda e, o=slot[:, 0:n4 * 128], i=ps_t[:, 0:n4 * 128]:
                               e.activation(o, i, AF.Copy),
                               waits=[("tensor", tk)] + eT_free[gi % 2], inc=1)
                et_guard[bank] = [(eng, evt)]
                evts[gi] = (eng, evt)

            def emit_AV(gi):
                kb0, n4 = groups[gi]
                slot = cwork[:, (gi % 2) * 512:(gi % 2) * 512 + 512]
                tk = 0
                for u in range(n4):
                    kb = kb0 + u
                    ktile = 8 * p + kb
                    w0 = [evts[gi]] if u == 0 else []
                    if kb == 0:
                        w0 = w0 + oh_ps_guard.get(4 + ii % 2, [])
                    tk = P.do("tensor", lambda e, o=ps_o[:, 0:128],
                              a=slot[:, u * 128:(u + 1) * 128],
                              b=vbuf[:, ktile * 128:(ktile + 1) * 128],
                              s=(kb == 0), q=(kb == nkb - 1):
                              e.matmul(o, a, b, start=s, stop=q),
                              waits=w0, inc=(1 if u == n4 - 1 else 0))
                eT_free[gi % 2] = [("tensor", tk)]

            for gi in range(len(groups)):
                emit_T(gi)
                if gi > 0:
                    emit_AV(gi - 1)
            emit_AV(len(groups) - 1)
            av_tok = P.tok("tensor")
            prev_epan_rd[par] = av_tok
            pv["av"] = av_tok

        def av_stage2(pv, iz=iz):
            ii, j = pv["ii"], pv["j"]
            par = pv["par"]
            sc0 = 8 * par
            ps_o = mk.ps[4 + ii % 2]
            mg = P.do(V, lambda e, o=ohacc[:, j * 128:(j + 1) * 128],
                      i=ohacc[:, j * 128:(j + 1) * 128], a=astat[:, sc0 + 2:sc0 + 3],
                      b=ps_o[:, 0:128]:
                      e.scalar_tensor_tensor(o, i, a, b, ALU.mult, ALU.add),
                      waits=[("tensor", pv["av"]), ("scalar", pv["al"]), ("vector", iz)],
                      inc=1)
            sj = 48 + j
            P.do(V, lambda e, o=astat[:, sj:sj + 1], i=astat[:, sj:sj + 1],
                 a=astat[:, sc0 + 2:sc0 + 3], b=astat[:, sc0 + 4:sc0 + 5]:
                 e.scalar_tensor_tensor(o, i, a, b, ALU.mult, ALU.add),
                 waits=[("scalar", pv["ex"])])
            P.drain(V, inc=1)
            oh_ps_guard[4 + ii % 2] = [("vector", P.tok(V))]

        npan = (q0 + GW) // 1024
        its = []
        for p in range(npan):
            for j in range(GQT):
                jt = g * GQT + j
                kext = min(1024, 128 * (jt + 1) - 1024 * p)
                if kext > 0:
                    its.append((p, j, jt, kext))

        def load_panel(pp):
            sl2 = pp % 2
            cg = cpan_guard[sl2] + [("tensor", P.tok("tensor"))]
            for lc in range(LCH):
                for hl, dstp in ((0, cpan[sl2]["hi"]), (1, cpan[sl2]["lo"])):
                    P.do("sync", lambda e, o=dstp[:, lc * 1024:(lc + 1) * 1024],
                         i=dtb["cbuf"][hl, lc][:, pp * 1024:(pp + 1) * 1024]:
                         e.dma_start(out=o, in_=i),
                         waits=[(cbw, cb_done)] + cg, inc=16, sem=cpsem[sl2])
            pan_tok[pp] = cpsem[sl2].cnt

        pan_tok = {}
        load_panel(0)
        cur_p = -1
        prev = None
        for ii, (p, j, jt, kext) in enumerate(its):
            if p != cur_p:
                cur_p = p
                if p + 1 < npan:
                    load_panel(p + 1)
            sl = p % 2
            par = ii % 2
            sc0 = 8 * par
            spv = span if par == 0 else span2
            epv = epan if par == 0 else epan2
            nkh = (kext + 511) // 512
            khw = [min(512, kext - ki * 512) for ki in range(nkh)]
            sc_w = [(cpsem[sl], pan_tok[p])] + sc_guard.get(0, []) + sc_guard.get(1, [])
            for lc in range(LCH):
                qh = qt_hi[:, lc * GW + j * 128:lc * GW + (j + 1) * 128]
                ql = qt_lo[:, lc * GW + j * 128:lc * GW + (j + 1) * 128]
                for ki in range(nkh):
                    kh = ki * 512
                    w = khw[ki]
                    P.do("tensor", lambda e, o=mk.ps[ki][:, 0:w],
                         a=qh, b=cpan[sl]["hi"][:, lc * 1024 + kh:lc * 1024 + kh + w],
                         s=(lc == 0): e.matmul(o, a, b, start=s, stop=False),
                         waits=sc_w)
                    sc_w = ()
                    P.do("tensor", lambda e, o=mk.ps[ki][:, 0:w],
                         a=qh, b=cpan[sl]["lo"][:, lc * 1024 + kh:lc * 1024 + kh + w]:
                         e.matmul(o, a, b, start=False, stop=False))
                for ki in range(nkh):
                    kh = ki * 512
                    w = khw[ki]
                    P.do("tensor", lambda e, o=mk.ps[ki][:, 0:w],
                         a=ql, b=cpan[sl]["hi"][:, lc * 1024 + kh:lc * 1024 + kh + w],
                         s=(lc == LCH - 1): e.matmul(o, a, b, start=False, stop=s),
                         inc=(1 if (lc == LCH - 1 and ki == nkh - 1) else 0))
            stk_ = P.tok("tensor")
            if prev is not None:
                av_stage(prev)
            for ki in range(nkh):
                kh = ki * 512
                w = khw[ki]
                mc = astat[:, sc0:sc0 + 1]
                if ki == 0:
                    ev = P.do(V, lambda e, o=spv[:, kh:kh + w], i=mk.ps[ki][:, 0:w], m=mc:
                              e.tensor_scalar(o, i, 0.0, None, ALU.add, ALU.max, accum_out=m),
                              waits=[("tensor", stk_), ("scalar", prev_exp.get(par, 0))], inc=1)
                else:
                    P.drain(V)
                    ev = P.do(V, lambda e, o=spv[:, kh:kh + w], i=mk.ps[ki][:, 0:w], m=mc:
                              e.tensor_scalar(o, i, 0.0, m, ALU.add, ALU.max, accum_out=m),
                              inc=1)
                sc_guard[ki] = [("vector", ev)]
            if 128 * (jt + 1) <= 1024 * (p + 1):
                P.drain(V)
                dcol = 128 * jt - 1024 * p
                P.do(V, lambda e, o=spv[:, dcol:dcol + 128], a=spv[:, dcol:dcol + 128],
                     b=mk.tril: e.tensor_tensor(o, a, b, ALU.add))
            P.drain(V)
            mo = (16 + j if mslot[j] == 0 else 32 + j)
            mn = (32 + j if mslot[j] == 0 else 16 + j)
            P.do(V, lambda e, o=astat[:, mn:mn + 1], a=astat[:, mo:mo + 1],
                 b=astat[:, sc0:sc0 + 1]: e.tensor_tensor(o, a, b, ALU.max))
            P.drain(V)
            P.do(V, lambda e, o=astat[:, sc0 + 2:sc0 + 3], a=astat[:, mo:mo + 1],
                 b=astat[:, mn:mn + 1]: e.tensor_tensor(o, a, b, ALU.subtract))
            P.do(V, lambda e, o=astat[:, sc0 + 3:sc0 + 4], i=astat[:, mn:mn + 1]:
                 e.tensor_scalar(o, i, -1.0, None, ALU.mult))
            P.drain(V, inc=1)
            mtok = P.tok(V)
            al_tok = P.do("scalar", lambda e, o=astat[:, sc0 + 2:sc0 + 3],
                          i=astat[:, sc0 + 2:sc0 + 3]:
                          e.activation(o, i, AF.Exp), waits=[("vector", mtok)], inc=1)
            ex_tok = P.do("scalar", lambda e, o=epv[:, 0:kext], i=spv[:, 0:kext],
                          b=astat[:, sc0 + 3:sc0 + 4], s=astat[:, sc0 + 4:sc0 + 5]:
                          e.activation(o, i, AF.Exp, bias=b, accum_out=s),
                          waits=[("vector", mtok), ("tensor", prev_epan_rd.get(par, 0))],
                          inc=1)
            prev_exp[par] = ex_tok
            if prev is not None:
                av_stage2(prev)
            prev = dict(ii=ii, p=p, j=j, kext=kext, ex=ex_tok, al=al_tok, par=par)
            mslot[j] ^= 1
        if prev is not None:
            av_stage(prev)
            av_stage2(prev)
        oh_done = P.drain(V, inc=1)
        P.do(V, lambda e, o=astat[:, 0:GQT], i=astat[:, 48:48 + GQT]: e.reciprocal(o, i),
             waits=[("vector", oh_done)])
        rc_tok = P.drain(V, inc=1)
        ohT_hi = csh
        ohT_lo = csl
        spl_tok = []
        for gi in range((GQT + 3) // 4):
            n4 = min(4, GQT - gi * 4)
            bank = 2 + gi % 2
            ps_t = mk.ps[bank]
            tk = 0
            for u in range(n4):
                j = gi * 4 + u
                tk = P.do("tensor", lambda e, o=ps_t[:, u * 128:(u + 1) * 128],
                          a=ohacc[:, j * 128:(j + 1) * 128], b=mk.ident,
                          s=(u == 0), q=(u == n4 - 1):
                          e.matmul(o, a, b, is_transpose=True, start=s, stop=q),
                          waits=([("vector", oh_done)] + et_guard.get(bank, []))
                          if u == 0 else [], inc=(1 if u == n4 - 1 else 0))
            hs = P.do("scalar", lambda e, o=ohT_hi[:, gi * 512:gi * 512 + n4 * 128],
                      i=ps_t[:, 0:n4 * 128]: e.activation(o, i, AF.Copy),
                      waits=[("tensor", tk), ("scalar", mk.cstage_sfree.get(hp, 0))], inc=1)
            ls = P.do(V, lambda e, o=ohT_lo[:, gi * 512:gi * 512 + n4 * 128],
                      a=ps_t[:, 0:n4 * 128], b=ohT_hi[:, gi * 512:gi * 512 + n4 * 128]:
                      e.tensor_tensor(o, a, b, ALU.subtract),
                      waits=[("scalar", hs)], inc=1)
            et_guard[bank] = [("vector", ls), ("scalar", hs)]
            spl_tok.append((hs, ls))
        hbuf = fsS
        hb_tok = 0
        for j in range(GQT):
            gi = j // 4
            hs, ls = spl_tok[gi]
            ps_2 = mk.ps[6 + j % 2]
            mm3(P, ps_2[:, 0:128], ohT_hi[:, j * 128:(j + 1) * 128],
                ohT_lo[:, j * 128:(j + 1) * 128],
                whi[:, 1536:1664], wlo[:, 1536:1664], first=True, last=True,
                waits=[("scalar", hs), ("vector", ls)] + mk.attn_ps_guard.get(6 + j % 2, []),
                inc=1)
            o2_tok = P.tok("tensor")
            hb_tok = P.do(V, lambda e, o=hbuf[:, j * 128:(j + 1) * 128], i=ps_2[:, 0:128],
                          s=astat[:, j:j + 1]:
                          e.tensor_scalar(o, i, s, None, ALU.mult),
                          waits=[("tensor", o2_tok), ("vector", rc_tok)], inc=1)
            mk.attn_ps_guard[6 + j % 2] = [("vector", hb_tok)]
        hb_done = P.drain(V, inc=1)
        HG = min(8, GQT)
        for gi in range((GQT + 3) // 4):
            n4 = min(4, GQT - gi * 4)
            bank = 2 + gi % 2
            ps_t = mk.ps[bank]
            tk = 0
            for u in range(n4):
                j = gi * 4 + u
                tk = P.do("tensor", lambda e, o=ps_t[:, u * 128:(u + 1) * 128],
                          a=hbuf[:, j * 128:(j + 1) * 128], b=mk.ident,
                          s=(u == 0), q=(u == n4 - 1):
                          e.matmul(o, a, b, is_transpose=True, start=s, stop=q),
                          waits=([("vector", hb_done)] + et_guard.get(bank, []))
                          if u == 0 else [], inc=(1 if u == n4 - 1 else 0))
            sc = (gi * 512) % (HG * 128)
            f2h = P.do("scalar", lambda e, o=o2st[:, sc:sc + n4 * 128],
                       i=ps_t[:, 0:n4 * 128]: e.activation(o, i, AF.Copy),
                       waits=[("tensor", tk), (ag2w, prev_ag2)], inc=1)
            f2l = P.do(V, lambda e, o=o2st[:, HG * 128 + sc:HG * 128 + sc + n4 * 128],
                       a=ps_t[:, 0:n4 * 128], b=o2st[:, sc:sc + n4 * 128]:
                       e.tensor_tensor(o, a, b, ALU.subtract),
                       waits=[("scalar", f2h)], inc=1)
            et_guard[bank] = [("vector", f2l), ("scalar", f2h)]
            if (gi + 1) % 2 == 0:
                qb = q0 + (gi - 1) * 512
                stg_tok2 = [("scalar", f2h), ("vector", f2l)]
                for hl in range(2):
                    P.do("gpsimd", lambda e, o=dtb["o2buf"][hp, hl][:, qb:qb + HG * 128],
                         i=o2st[:, hl * HG * 128:(hl + 1) * HG * 128]:
                         e.dma_start(out=o, in_=i),
                         waits=stg_tok2, inc=16, sem=ag2w)
                prev_ag2 = ag2w.cnt
        mk.cstage_sfree[hp] = P.tok("scalar")
    done = [(eng, P.tok(eng)) for eng in ("tensor", "vector", "scalar")]
    mk.attn_head_guard[hp + 1] = done
    mk.attn_xh_guard = [("tensor", P.tok("tensor")), mk.last_xhld]
    mk.attn_qt_guard = [("tensor", P.tok("tensor"))]
    mk.attn_prev_pan = [(cpsem[0], cpsem[0].cnt), (cpsem[1], cpsem[1].cnt)]


def _emit_wo_partial(mk, R, o2w, coll_cb):
    P, cfg = mk.P, mk.cfg
    T = cfg["T"]; D = cfg["D"]; S = cfg["S"]
    V = "vector"
    dtb = mk.dt
    o2sl = R["cstage"]
    wot_hi = R["cpan"][0]["hi"]
    wot_lo = R["cpan"][0]["lo"]
    wot_hi2 = R["cpan"][1]["hi"]
    wot_lo2 = R["cpan"][1]["lo"]
    wraw = R["wwork"]
    done = mk.attn_head_guard[2]
    wsem = P.xsem("wops")
    sp_tok = []
    for hpi, (wh, wl) in enumerate(((wot_hi, wot_lo), (wot_hi2, wot_lo2))):
        for pc in range(D // 512):
            dtok = P.do("sync", lambda e, o=wraw[:, 0:512] if pc % 2 == 0 else wraw[:, 512:1024],
                        i=dtb["woT"][hpi][:, pc * 512:(pc + 1) * 512]:
                        e.dma_start(out=o, in_=i),
                        waits=done + (sp_tok[-2:] if len(sp_tok) >= 2 else []), inc=16, sem=wsem)
            rr = wraw[:, 0:512] if pc % 2 == 0 else wraw[:, 512:1024]
            ht = P.do("scalar", lambda e, o=wh[:, pc * 512:(pc + 1) * 512], i=rr:
                      e.activation(o, i, AF.Copy), waits=[(wsem, dtok)] + done, inc=1)
            lt = P.do(V, lambda e, o=wl[:, pc * 512:(pc + 1) * 512], a=rr,
                      b=wh[:, pc * 512:(pc + 1) * 512]:
                      e.tensor_tensor(o, a, b, ALU.subtract), waits=[("scalar", ht)], inc=1)
            sp_tok.append(("scalar", ht))
            sp_tok.append(("vector", lt))
    w_ready = [("scalar", P.tok("scalar")), ("vector", P.tok("vector"))]
    mk.wop_w_ready = w_ready
    NEO = max(1, D // 512)
    if NEO <= 2:
        eo_passes = [tuple(range(NEO))]
    else:
        eo_passes = [tuple(range(NEO // 2)), tuple(range(NEO // 2, NEO))]
    npass = len(eo_passes)
    assert npass == cfg["NRSH"]
    EW = 512 * len(eo_passes[0])
    SLOT = 4 * EW
    fs = R["fs"]
    assert S + 2 * SLOT <= mk.fs_width
    stg_slots = [fs[:, S + i * SLOT: S + (i + 1) * SLOT] for i in range(2)]
    mk.wof_late_cols = (S, S + 2 * SLOT) if mk.fs_aliased else None
    osem = [P.xsem("o2a"), P.xsem("o2b")]
    o_guard = [[], []]
    bank_free = {}
    rsp = [P.xsem(f"rsp{i}") for i in range(npass)]
    stg_guard = {0: [], 1: []}
    setsel = 0
    mk.rs_ready = []
    for pi, eos in enumerate(eo_passes):
        oq = "sync" if pi % 2 == 0 else "scalar"
        for tt in range(S // 512):
            sl = tt % 2
            stgb = stg_slots[sl]
            otok = 0
            for hpi in range(2):
                for hl in range(2):
                    off = sl * 2048 + hpi * 1024 + hl * 512
                    otok = P.do(oq, lambda e, o=o2sl[:, off:off + 512],
                                i=dtb["o2buf"][hpi, hl][:, tt * 512:(tt + 1) * 512]:
                                e.dma_start(out=o, in_=i),
                                waits=[(o2w, o2w.cnt)] + o_guard[sl] + done, inc=16, sem=osem[sl])
            ev = 0
            for sub in range(4):
                bs = setsel * 4
                for ei, eo in enumerate(eos):
                    bank = bs + ei
                    mmw = [(osem[sl], otok)] + w_ready + bank_free.get(bank, []) + o_guard[sl]
                    for hpi in range(2):
                        wh = (wot_hi, wot_hi2)[hpi]
                        wl = (wot_lo, wot_lo2)[hpi]
                        off = sl * 2048 + hpi * 1024
                        mm3(P, mk.ps[bank],
                            o2sl[:, off + sub * 128: off + (sub + 1) * 128],
                            o2sl[:, off + 512 + sub * 128: off + 512 + (sub + 1) * 128],
                            wh[:, eo * 512:(eo + 1) * 512], wl[:, eo * 512:(eo + 1) * 512],
                            first=(hpi == 0), last=(hpi == 1), waits=mmw,
                            inc=(1 if hpi == 1 else 0))
                        mmw = ()
                    t_tok = P.tok("tensor")
                    ev = P.do(V, lambda e, o=stgb[:, sub * EW + ei * 512: sub * EW + ei * 512 + 512],
                              i=mk.ps[bank]: e.tensor_copy(o, i),
                              waits=[("tensor", t_tok)] + stg_guard[sl], inc=1)
                    bank_free[bank] = [("vector", ev)]
                setsel ^= 1
            dtk = P.do("gpsimd", lambda e,
                       o=dtb["rs_in"][pi][tt * 512:(tt + 1) * 512, 0:EW].rearrange(
                           "(s p) c -> p s c", p=128),
                       i=stgb.rearrange("p (s c) -> p s c", s=4):
                       e.dma_start(out=o, in_=i),
                       waits=[("vector", ev), (rsp[pi], rsp[pi].cnt)],
                       inc=16, sem=rsp[pi])
            stg_guard[sl] = [(rsp[pi], dtk)]
            o_guard[sl] = [("tensor", P.tok("tensor"))]
        mk.rs_ready.append((rsp[pi], rsp[pi].cnt))
        coll_cb(pi)
    mk.rs_done = [(s, s.cnt) for s in rsp]
    mk.wop_bank_free = bank_free
    mk.wop_done_v = ("vector", P.tok("vector"))


def emit_wo_final(mk):
    P, cfg = mk.P, mk.cfg
    T = cfg["T"]; D = cfg["D"]; NCH = D // 128
    V = "vector"
    dtb = mk.dt
    af32, abf = mk.af32, mk.abf
    af32.push()
    tmp = af32.take(2 * D, "wof_tm")
    bo_sb = af32.take(NCH, "bo")
    head2 = mk.attn_head_guard[2]
    late = set()
    if mk.wof_late_cols is not None:
        lo, hi = mk.wof_late_cols
        late = {c for c in range(NCH) if c * T < hi and (c + 1) * T > lo}
    xld_e, xld_l = P.xsem("xlde"), P.xsem("xldl")
    xsem_of = {}
    for c in range(NCH):
        if c in late:
            w = [(mk.spx, mk.spx.cnt)] + head2 + mk.rs_done
            sem = xld_l
        else:
            w = [(mk.spx, mk.spx.cnt)] + head2
            sem = xld_e
        P.do("gpsimd", lambda e, o=mk.xres[:, c * T:(c + 1) * T], i=dtb["xspill"][c]:
             e.dma_start(out=o, in_=i), waits=w, inc=16, sem=sem)
        xsem_of[c] = sem
    bld = P.xsem("bld")
    P.do("sync", lambda e, o=bo_sb[:, 0:NCH],
         i=dtb["bo"].rearrange("(c p) -> p c", p=128):
         e.dma_start(out=o, in_=i, allow_slow_non_contiguous=True),
         waits=head2 + mk.rs_done + [mk.wop_done_v], inc=16, sem=bld)
    NRSH = cfg["NRSH"]
    W = D // NRSH
    nsl = min(4, (2 * D) // W)
    rsl = [P.xsem(f"rsl{i}") for i in range(nsl)]
    bank_g = {}
    slot_g = {i: [] for i in range(nsl)}
    nq = T // 128
    for hf in range(NRSH):
        cc_need = hf + 1
        for q in range(nq):
            sidx = (hf * nq + q) % nsl
            qe = "sync" if q % 2 == 0 else "scalar"
            sem = rsl[sidx]
            tv = tmp[:, sidx * W: sidx * W + W]
            dtok = P.do(qe, lambda e, o=tv,
                        i=dtb["rs_out"][hf][q * 128:(q + 1) * 128, :]:
                        e.dma_start(out=o, in_=i),
                        waits=[(mk.cc2, cc_need)] + head2 + mk.wop_w_ready
                        + slot_g[sidx],
                        inc=16, sem=sem)
            tk = 0
            for ci in range(NCH // NRSH):
                c = hf * (NCH // NRSH) + ci
                bank = ci % 2
                ps = mk.ps[bank]
                tk = P.do("tensor", lambda e, o=ps[:, 0:128], i=tv[:, ci * 128:(ci + 1) * 128],
                          idn=mk.ident: e.transpose(o, i, idn),
                          waits=[(sem, dtok)] + bank_g.get(bank, mk.wop_bank_free.get(bank, [])),
                          inc=1)
                ev = P.do(V, lambda e, o=mk.xres[:, c * T + q * 128:c * T + (q + 1) * 128],
                          i=ps[:, 0:128], s=bo_sb[:, c:c + 1],
                          a=mk.xres[:, c * T + q * 128:c * T + (q + 1) * 128]:
                          e.scalar_tensor_tensor(o, i, s, a, ALU.add, ALU.add),
                          waits=[("tensor", tk), (xsem_of[c], xsem_of[c].cnt),
                                 (bld, bld.cnt)], inc=1)
                bank_g[bank] = [("vector", ev)]
            slot_g[sidx] = [("tensor", tk)]
    P.drain(V, inc=1)
    mk.x_ready = {"v": P.tok("vector"), "t": P.tok("tensor"), "s": P.tok("scalar")}
    af32.pop()


def emit_store_y(mk):
    P, cfg = mk.P, mk.cfg
    T = cfg["T"]; D = cfg["D"]; NCH = D // 128
    nq = T // 128
    af32 = mk.af32
    af32.push()
    tmp = af32.take(2 * D, "ytm")
    xr = mk.x_ready
    sems = [P.xsem("ya"), P.xsem("yb")]
    prev_dma = [0, 0]
    ev_tok = {0: 0, 1: 0}
    for q in range(nq):
        slot = q % 2
        tmv = tmp[:, slot * D:(slot + 1) * D]
        last_ev = 0
        for c in range(NCH):
            ps = mk.ps[c % 2]
            tk = P.do("tensor", lambda e, o=ps[:, 0:128],
                      i=mk.xres[:, c * T + q * 128: c * T + (q + 1) * 128], idn=mk.ident:
                      e.transpose(o, i, idn),
                      waits=[("vector", xr["v"]), ("vector", ev_tok[c % 2])], inc=1)
            last_ev = P.do("vector", lambda e, o=tmv[:, c * 128:(c + 1) * 128], i=ps[:, 0:128]:
                           e.tensor_copy(o, i),
                           waits=[("tensor", tk), (sems[slot], prev_dma[slot])], inc=1)
            ev_tok[c % 2] = last_ev
        prev_dma[slot] = P.do("sync", lambda e, o=mk.dt["y"][q * 128:(q + 1) * 128, :], i=tmv:
                              e.dma_start(out=o, in_=i), waits=[("vector", last_ev)], inc=16,
                              sem=sems[slot])
    P.do("sync", lambda e: e.nop(),
         waits=[(sems[0], sems[0].cnt), (sems[1], sems[1].cnt)])
    af32.pop()


from concourse.bass_utils import run_bass_kernel_spmd
import ml_dtypes

_NCORES = 8
_S, _D = 8192, 2048
_T = _S // _NCORES
_CFG = dict(T=_T, D=_D, S=_S, DE=1408, DS=2816, NEXP=8,
            phases=("ffn1", "attn", "ffn2"), n_cores=_NCORES)

_NC_CACHE = {}


def _build_once():
    if "nc" not in _NC_CACHE:
        _NC_CACHE["nc"] = build(_CFG)
    return _NC_CACHE["nc"]


def kernel(L0_attn_x, L0_attn_Wq, L0_attn_Wdkv, L0_attn_Wuk, L0_attn_Wuv,
           L0_attn_ln_g, L0_attn_ln_b, L0_attn_Wo_h, L0_attn_Wo, L0_attn_bo,
           L0_ffn_W_router, L0_ffn_We_gate, L0_ffn_We_up, L0_ffn_We_down,
           L0_ffn_Ws_gate, L0_ffn_Ws_up, L0_ffn_Ws_down):
    nc, mk_obj = _build_once()
    x = np.ascontiguousarray(np.asarray(L0_attn_x, dtype=np.float32).reshape(_S, _D))
    wo = np.asarray(L0_attn_Wo, dtype=np.float32)
    consts = {
        "ident": np.eye(128, dtype=np.float32),
        "identb": np.eye(128, dtype=ml_dtypes.bfloat16),
        "tril": np.where(np.tril(np.ones((128, 128), bool)), 0.0, -1e30).astype(np.float32),
        "esel": np.stack([np.repeat(np.eye(8, dtype=np.float32)[:, e:e + 1], 128, axis=1)
                          for e in range(8)]),
    }
    shared = {
        "wr": np.asarray(L0_ffn_W_router, dtype=np.float32),
        "wsg": np.asarray(L0_ffn_Ws_gate, dtype=np.float32),
        "wsu": np.asarray(L0_ffn_Ws_up, dtype=np.float32),
        "wsd": np.asarray(L0_ffn_Ws_down, dtype=np.float32),
        "weg": np.asarray(L0_ffn_We_gate, dtype=np.float32),
        "weu": np.asarray(L0_ffn_We_up, dtype=np.float32),
        "wed": np.asarray(L0_ffn_We_down, dtype=np.float32),
        "wo": None,
        "bo": np.asarray(L0_attn_bo, dtype=np.float32),
    }
    del shared["wo"]
    wq = np.asarray(L0_attn_Wq, dtype=np.float32)
    wdkv = np.asarray(L0_attn_Wdkv, dtype=np.float32)
    wuk = np.asarray(L0_attn_Wuk, dtype=np.float32)
    wuv = np.asarray(L0_attn_Wuv, dtype=np.float32)
    woh = np.asarray(L0_attn_Wo_h, dtype=np.float32)
    lng = np.asarray(L0_attn_ln_g, dtype=np.float32)
    lnb = np.asarray(L0_attn_ln_b, dtype=np.float32)
    in_maps = []
    for r in range(_NCORES):
        hs = slice(2 * r, 2 * r + 2)
        ins = dict(consts)
        ins.update(shared)
        ins.update({
            "x0": np.ascontiguousarray(x[r * _T:(r + 1) * _T]),
            "wq": np.ascontiguousarray(wq[hs]),
            "wdkv": np.ascontiguousarray(wdkv[hs]),
            "wuk": np.ascontiguousarray(wuk[hs]),
            "wuv": np.ascontiguousarray(wuv[hs]),
            "woh": np.ascontiguousarray(woh[hs]),
            "lngr": np.ascontiguousarray(np.repeat(lng[hs][:, None, :], 128, axis=1)),
            "lnbr": np.ascontiguousarray(np.repeat(lnb[hs][:, None, :], 128, axis=1)),
            "woT": np.ascontiguousarray(
                wo[:, 256 * r:256 * r + 256].T.reshape(2, 128, _D)),
            "ridx": np.array([[r]], np.int32),
        })
        in_maps.append(ins)
    res = run_bass_kernel_spmd(nc, in_maps, core_ids=list(range(_NCORES)))
    y = np.concatenate([np.asarray(res.results[r]["y"]) for r in range(_NCORES)], axis=0)
    return y.reshape(1, _S, _D).astype(np.float32)
